# version 02 (final)
# baseline (speedup 1.0000x reference)
import numpy as np
import concourse.bass as bass
import concourse.mybir as mybir
from concourse.bass_utils import run_bass_kernel_spmd

F32 = mybir.dt.float32
BF16 = mybir.dt.bfloat16
AF = mybir.ActivationFunctionType
ALU = mybir.AluOpType
AX = mybir.AxisListType

ENGS = ("sync", "tensor", "vector", "scalar", "gpsimd")

NCORE = 8
B, S, D = 4, 2048, 2048
TOK = 1024
KT = 16
NE = 8
FF = 1408
FT = 11
NST = 16
H1, DK1, DV1 = 6, 256, 512


RACES = True


class Prog:

    AUTO = ("vector", "scalar", "gpsimd")

    def __init__(self):
        self.nc = bass.Bass("TRN2", target_bir_lowering=False, debug=False,
                            detect_race_conditions=RACES)
        self.q = {e: [] for e in ENGS}
        self.icnt = {e: 0 for e in ENGS}
        self._isems = {}
        self._sems = {}
        self.cnt = {}
        self.vs = {}
        self._barsem = None
        self._barcnt = 0

    def _isem(self, eng):
        if eng not in self._isems:
            self._isems[eng] = self.nc.alloc_semaphore(f"I_{eng}")
        return self._isems[eng]

    def sem(self, name):
        if name not in self._sems:
            self._sems[name] = self.nc.alloc_semaphore(name)
            self.cnt[name] = 0
        return self._sems[name]

    def _push(self, eng, f, inc, inc_n):
        isem = self._isem(eng)
        if eng in self.AUTO and self.icnt[eng] > 0:
            v = self.icnt[eng]
            self.q[eng].append(lambda e, s=isem, v=v: e.wait_ge(s, v))
        self.icnt[eng] += 1
        if inc is not None:
            self.vs.setdefault(inc, {})[eng] = self.icnt[eng]

        def run(e, f=f, s=isem):
            f(e).then_inc(s, 1)
        self.q[eng].append(run)

    def wait(self, eng, semname, val=None):
        if semname in self._sems or (val is not None
                                     and not isinstance(val, dict)):
            self.sem(semname)
            v = self.cnt[semname] if val is None else val
            s = self._sems[semname]
            self.q[eng].append(lambda e, s=s, v=v: e.wait_ge(s, v))
            return
        snap = val if isinstance(val, dict) else dict(self.vs.get(semname, {}))
        for pe, cntv in snap.items():
            if pe == eng and pe in self.AUTO:
                continue
            s = self._isem(pe)
            self.q[eng].append(lambda e, s=s, v=cntv: e.wait_ge(s, v))

    def val(self, semname):
        if semname in self._sems:
            return self.cnt[semname]
        return dict(self.vs.get(semname, {}))

    def wait_eng(self, eng, prod):
        if self.icnt[prod] > 0 and not (eng == prod and prod in self.AUTO):
            s = self._isem(prod)
            v = self.icnt[prod]
            self.q[eng].append(lambda e, s=s, v=v: e.wait_ge(s, v))

    def dma(self, eng, out, in_, sem="ld", accum=False):
        self.sem(sem)
        self.cnt[sem] += 16
        s = self._sems[sem]
        if accum:
            self.q[eng].append(lambda e, o=out, i=in_, s=s: e.dma_start(
                out=o, in_=i, accum_op=ALU.add).then_inc(s, 16))
        else:
            self.q[eng].append(lambda e, o=out, i=in_, s=s: e.dma_start(
                out=o, in_=i).then_inc(s, 16))

    def mm(self, psum, lhsT, rhs, start, stop, inc=None):
        self._push("tensor", lambda e, p=psum, l=lhsT, r=rhs, a=start, b=stop:
                   e.matmul(p, l, r, start=a, stop=b), inc, 1)

    def tp(self, psum, in_, ident, inc=None):
        self._push("tensor", lambda e, p=psum, i=in_, d=ident:
                   e.transpose(p, i, d), inc, 1)

    def act(self, out, in_, func, bias=0.0, scale=1.0, inc=None):
        self._push("scalar", lambda e, o=out, i=in_, f=func, b=bias, s=scale:
                   e.activation(o, i, f, bias=b, scale=s), inc, 1)

    def tt(self, eng, out, a, b, op, inc=None):
        self._push(eng, lambda e, o=out, x=a, y=b, p=op:
                   e.tensor_tensor(o, x, y, p), inc, 1)

    def ts(self, eng, out, a, s, op, inc=None):
        self._push(eng, lambda e, o=out, x=a, sc=s, p=op:
                   e.tensor_single_scalar(o, x, sc, p), inc, 1)

    def stt(self, eng, out, a, s, b, op0, op1, inc=None):
        self._push(eng, lambda e, o=out, x=a, sc=s, y=b, p0=op0, p1=op1:
                   e.scalar_tensor_tensor(o, x, sc, y, p0, p1), inc, 1)

    def cp(self, eng, out, in_, inc=None):
        if eng == "scalar":
            self._push(eng, lambda e, o=out, i=in_: e.copy(o, i), inc, 1)
        else:
            self._push(eng, lambda e, o=out, i=in_: e.tensor_copy(o, i),
                       inc, 1)

    def red(self, eng, out, in_, op, inc=None):
        self._push(eng, lambda e, o=out, i=in_, p=op:
                   e.tensor_reduce(o, i, AX.X, p), inc, 1)

    def scan(self, out, d0, d1, init, inc=None):
        self._push("vector", lambda e, o=out, a=d0, b=d1, i=init:
                   e.tensor_tensor_scan(o, a, b, i, ALU.mult, ALU.add), inc, 1)

    def recip(self, out, in_, inc=None):
        self._push("vector", lambda e, o=out, i=in_: e.reciprocal(o, i),
                   inc, 1)

    def rmax(self, out, in_, inc=None):
        self._push("vector", lambda e, o=out, i=in_:
                   e.reduce_max(o, i, AX.X), inc, 1)

    def mset(self, ap, v, inc=None):
        self._push("gpsimd", lambda e, a=ap, x=v: e.memset(a, x), inc, 1)

    def asel(self, out, in_, pattern, cmp, fill, base, cm, inc=None):
        self._push("gpsimd", lambda e, o=out, i=in_, p=pattern, c=cmp, f=fill,
                   b=base, m=cm: e.affine_select(o, i, p, c, f, base=b,
                                                 channel_multiplier=m), inc, 1)

    def nop(self, eng, inc=None):
        self._push(eng, lambda e: e.nop(), inc, 1)

    def barrier(self, engs=ENGS):
        if self._barsem is None:
            self._barsem = self.nc.alloc_semaphore("bar")
        self._barcnt += len(engs)
        t = self._barcnt
        s = self._barsem
        for eng in engs:
            if self.icnt[eng] > 0:
                v = self.icnt[eng]
                self.q[eng].append(
                    lambda e, s2=self._isem(eng), v=v: e.wait_ge(s2, v))
            self.q[eng].append(lambda e, s=s: e.sem_inc(s, 1))
            self.q[eng].append(lambda e, s=s, t=t: e.wait_ge(s, t))

    def finalize(self):
        with self.nc.Block() as block:
            for eng in ENGS:
                def body(E, eng=eng):
                    for f in self.q[eng]:
                        f(E)
                getattr(block, eng)(body)
        self.nc.reset()
        return self.nc


def alloc_common(P):
    nc = P.nc
    c = {}
    c["ones_col"] = nc.alloc_sbuf_tensor("ones_col", [128, 1], F32)
    c["ones_row"] = nc.alloc_sbuf_tensor("ones_row", [1, 128], F32)
    c["ident"] = nc.alloc_sbuf_tensor("ident", [128, 128], F32)
    c["invD"] = nc.alloc_sbuf_tensor("invD", [1, 1], F32)
    c["eps_ln"] = nc.alloc_sbuf_tensor("eps_ln", [1, 1], F32)
    c["m05"] = nc.alloc_sbuf_tensor("m05", [1, 1], F32)
    c["half_col"] = nc.alloc_sbuf_tensor("half_col", [128, 1], F32)
    c["negbig_col"] = nc.alloc_sbuf_tensor("negbig_col", [128, 1], F32)
    c["zero_col"] = nc.alloc_sbuf_tensor("zero_col", [128, 1], F32)
    c["m05c"] = nc.alloc_sbuf_tensor("m05c", [128, 1], F32)
    c["epsc"] = nc.alloc_sbuf_tensor("epsc", [128, 1], F32)
    c["pb"] = [nc.alloc_psum_tensor(f"pb{i}", [128, 512], F32) for i in range(8)]
    P.mset(c["ones_col"][:], 1.0)
    P.mset(c["ones_row"][:], 1.0)
    P.mset(c["ident"][:], 1.0, inc="ge")
    P.wait("gpsimd", "ge", P.val("ge"))
    P.asel(c["ident"][:], c["ident"][:], [[-1, 128]], ALU.is_equal, 0.0, 0, 1)
    P.mset(c["invD"][:], 1.0 / D)
    P.mset(c["eps_ln"][:], 1e-5)
    P.mset(c["m05"][:], -0.5)
    P.mset(c["half_col"][:], 0.5)
    P.mset(c["negbig_col"][:], -1e30)
    P.mset(c["m05c"][:], -0.5)
    P.mset(c["epsc"][:], 1e-5)
    P.mset(c["zero_col"][:], 0.0, inc="ge")
    return c


TILT = 256


def emit_ln_stream(P, c, src_hbm, xr, xt, rows, tag):
    pb = c["pb"]
    mean_r, msq_r, var_r, rstd_r, mr_r = rows
    ntile = TOK // TILT
    ldx = [f"ldx0_{tag}", f"ldx1_{tag}"]
    nload = [0]
    consumed_evt = []

    def load_tile(i):
        k = nload[0]
        nload[0] += 1
        if k >= 2:
            sem, val = consumed_evt[k - 2]
            P.wait("sync", sem, val)
        P.dma("sync", xt[:, k % 2, :, :],
              src_hbm.rearrange("(a p) t -> p a t", p=128)[:, :, i * TILT:(i + 1) * TILT],
              sem=ldx[k % 2])

    for i in range(ntile):
        slot = i % 2
        load_tile(i)
        P.wait("tensor", ldx[i % 2], 16 * (i // 2 + 1))
        P.wait("tensor", "pe6", P.val("pe6"))
        for kt in range(KT):
            P.mm(pb[6][0:1, 0:TILT], c["ones_col"][:], xt[:, slot, kt, :],
                 kt == 0, kt == KT - 1, inc=("pf6" if kt == KT - 1 else None))
        P.wait("scalar", "pf6", P.val("pf6"))
        P.act(xt[:, slot, :, :].rearrange("p a t -> p (a t)"),
              xt[:, slot, :, :].rearrange("p a t -> p (a t)"), AF.Square,
              inc="sq")
        P.wait("tensor", "sq", P.val("sq"))
        P.wait("tensor", "pe7", P.val("pe7"))
        for kt in range(KT):
            P.mm(pb[7][0:1, 0:TILT], c["ones_col"][:], xt[:, slot, kt, :],
                 kt == 0, kt == KT - 1,
                 inc=("pf7" if kt == KT - 1 else None))
        consumed_evt.append(("pf7", P.val("pf7")))
        sl = slice(i * TILT, (i + 1) * TILT)
        P.wait("scalar", "pf6", P.val("pf6"))
        P.act(mean_r[0:1, sl], pb[6][0:1, 0:TILT], AF.Copy,
              scale=c["invD"][:], inc="pe6")
        P.wait("scalar", "pf7", P.val("pf7"))
        P.act(msq_r[0:1, sl], pb[7][0:1, 0:TILT], AF.Copy,
              scale=c["invD"][:], inc="pe7")
    P.wait("vector", "pe6", P.val("pe6"))
    P.wait("vector", "pe7", P.val("pe7"))
    P.tt("vector", var_r[0:1, :], mean_r[0:1, :], mean_r[0:1, :], ALU.mult)
    P.tt("vector", var_r[0:1, :], msq_r[0:1, :], var_r[0:1, :], ALU.subtract,
         inc="ve")
    P.wait("scalar", "ve", P.val("ve"))
    P.act(rstd_r[0:1, :], var_r[0:1, :], AF.Ln, bias=c["eps_ln"][:])
    P.act(rstd_r[0:1, :], rstd_r[0:1, :], AF.Exp, scale=c["m05"][:], inc="se")
    P.wait("vector", "se", P.val("se"))
    P.tt("vector", mr_r[0:1, :], mean_r[0:1, :], rstd_r[0:1, :], ALU.mult,
         inc="ve")
    P.wait("tensor", "se", P.val("se"))
    P.wait("tensor", "ve", P.val("ve"))
    for i in range(ntile):
        slot = i % 2
        load_tile(i)
        sl = slice(i * TILT, (i + 1) * TILT)
        P.wait("tensor", "pe6", P.val("pe6"))
        P.mm(pb[6][:, 0:TILT], c["ones_row"][:], rstd_r[0:1, sl], True, True,
             inc="pf6")
        P.wait("tensor", "pe7", P.val("pe7"))
        P.mm(pb[7][:, 0:TILT], c["ones_row"][:], mr_r[0:1, sl], True, True,
             inc="pf7")
        k2 = ntile + i
        P.wait("vector", ldx[k2 % 2], 16 * (k2 // 2 + 1))
        P.wait("vector", "pf6", P.val("pf6"))
        P.wait("vector", "pf7", P.val("pf7"))
        rb = pb[6][:, 0:TILT].unsqueeze(1).broadcast_to([128, KT, TILT])
        mb = pb[7][:, 0:TILT].unsqueeze(1).broadcast_to([128, KT, TILT])
        P.tt("vector", xt[:, slot, :, :], xt[:, slot, :, :], rb, ALU.mult,
             inc="pe6")
        P.tt("vector", xr[:, :, sl], xt[:, slot, :, :], mb, ALU.subtract,
             inc="pe7")
        consumed_evt.append(("pe7", P.val("pe7")))


class WStream:
    CAP = 22528

    def __init__(self, P, name, cap=None):
        self.P = P
        cap = cap or self.CAP
        self.bufs = [P.nc.alloc_sbuf_tensor(f"{name}{i}", [128, cap], BF16)
                     for i in range(2)]
        self.iu = 0
        self.rel_evts = []
        P.sem("wld0")
        P.sem("wld1")

    def load(self, hbm, nkt, m):
        P = self.P
        u = self.iu
        self.iu += 1
        buf = self.bufs[u % 2]
        sem = f"wld{u % 2}"
        if u >= 2:
            sem_e, val_e = self.rel_evts[u - 2]
            P.wait("gpsimd", sem_e, val_e)
        for kt in range(nkt):
            P.dma("gpsimd", buf[:, kt * m:(kt + 1) * m],
                  hbm[kt * 128:(kt + 1) * 128, 0:m], sem=sem)
        return {"buf": buf, "m": m, "nkt": nkt, "sem": sem,
                "tgt": P.val(sem), "u": u}

    def tile(self, unit, kt, mo, mlen):
        m = unit["m"]
        return unit["buf"][:, kt * m + mo:kt * m + mo + mlen]

    def consume_wait(self, unit):
        self.P.wait("tensor", unit["sem"], unit["tgt"])

    def release(self, evt):
        self.rel_evts.append(evt)


class StoreRot:
    def __init__(self, P, name, nslots=4, width=512):
        self.P = P
        self.stg = P.nc.alloc_sbuf_tensor(name, [128, nslots, width], F32)
        self.n = nslots
        self.w = width
        self.uses = 0
        self.slot_evt = [None] * nslots
        for i in range(nslots):
            P.sem(f"st{i}")

    def slot(self):
        return self.stg[:, self.uses % self.n, :]

    def pre_evac_wait(self):
        evt = self.slot_evt[self.uses % self.n]
        if evt is not None:
            self.P.wait("scalar", evt[0], evt[1])

    def store(self, hbm_out, evt, rows=128, width=None):
        P = self.P
        slot = self.uses % self.n
        w = width or self.w
        s = self.stg[:, slot, 0:w][0:rows, :]
        P.wait("sync", evt[0], evt[1])
        P.dma("sync", hbm_out, s, sem=f"st{slot}")
        self.slot_evt[slot] = (f"st{slot}", P.val(f"st{slot}"))
        self.uses += 1

    def wait_all(self, eng):
        for i in range(self.n):
            self.P.wait(eng, f"st{i}", self.P.val(f"st{i}"))


def emit_l0_routing(P, c, xr, gwT, bias_b, l_fm, c_fm, r_tm, s_tm, m_tm, w_tm,
                    r1, r2):
    pb = c["pb"]
    P.wait("tensor", "pe6", P.val("pe6"))
    P.mm(pb[6][:, 0:NE], c["ones_row"][:], bias_b[0:1, :], True, True,
         inc="pf6")
    P.wait("vector", "pf6", P.val("pf6"))
    P.cp("vector", bias_b[:], pb[6][:, 0:NE], inc="pe6")
    for tc in range(2):
        sl = slice(tc * 512, (tc + 1) * 512)
        P.wait("tensor", "pe6", P.val("pe6"))
        for kt in range(KT):
            P.mm(pb[6][0:NE, :], gwT[:, kt, :], xr[:, kt, sl], kt == 0,
                 kt == KT - 1, inc=("pf6" if kt == KT - 1 else None))
        P.wait("scalar", "pf6", P.val("pf6"))
        P.act(l_fm[:, sl], pb[6][0:NE, :], AF.Copy, inc="pe6")
    P.wait("tensor", "se", P.val("se"))
    P.wait("tensor", "pe6", P.val("pe6"))
    for ch in range(8):
        P.tp(pb[6][:, ch * NE:(ch + 1) * NE], l_fm[:, ch * 128:(ch + 1) * 128],
             c["ident"][0:NE, 0:NE], inc=("pf6" if ch == 7 else None))
    P.wait("vector", "pf6", P.val("pf6"))
    P.cp("vector", r_tm[:].rearrange("p a b -> p (a b)"), pb[6][:, 0:64],
         inc="pe6")
    P.wait("scalar", "pe6", P.val("pe6"))
    P.act(s_tm[:].rearrange("p a b -> p (a b)"),
          r_tm[:].rearrange("p a b -> p (a b)"), AF.Sigmoid, inc="se")
    P.wait("vector", "se", P.val("se"))
    bb = bias_b[:].unsqueeze(1).broadcast_to([128, 8, NE])
    P.tt("vector", r_tm[:], s_tm[:], bb, ALU.add, inc="ve")
    P.wait("vector", "ve", P.val("ve"))
    for ch in range(8):
        P.rmax(r1[:, ch:ch + 1], r_tm[:, ch, :],
               inc=("ve" if ch == 7 else None))
    P.wait("vector", "ve", P.val("ve"))
    for ch in range(8):
        P.ts("vector", m_tm[:, ch, :], r_tm[:, ch, :], r1[:, ch:ch + 1],
             ALU.is_ge)
    for ch in range(8):
        P.stt("vector", w_tm[:, ch, :], m_tm[:, ch, :], c["negbig_col"][:],
              r_tm[:, ch, :], ALU.mult, ALU.add,
              inc=("ve" if ch == 7 else None))
    P.wait("vector", "ve", P.val("ve"))
    for ch in range(8):
        P.rmax(r2[:, ch:ch + 1], w_tm[:, ch, :],
               inc=("ve" if ch == 7 else None))
    P.wait("vector", "ve", P.val("ve"))
    for ch in range(8):
        P.ts("vector", m_tm[:, ch, :], r_tm[:, ch, :], r2[:, ch:ch + 1],
             ALU.is_ge, inc=("ve" if ch == 7 else None))
    P.tt("vector", w_tm[:].rearrange("p a b -> p (a b)"),
         s_tm[:].rearrange("p a b -> p (a b)"),
         m_tm[:].rearrange("p a b -> p (a b)"), ALU.mult)
    for ch in range(8):
        P.red("vector", r1[:, ch:ch + 1], w_tm[:, ch, :], ALU.add,
              inc=("ve" if ch == 7 else None))
    P.wait("vector", "ve", P.val("ve"))
    P.recip(r2[:, 0:8], r1[:, 0:8], inc="ve")
    P.wait("vector", "ve", P.val("ve"))
    for ch in range(8):
        P.ts("vector", m_tm[:, ch, :], w_tm[:, ch, :], r2[:, ch:ch + 1],
             ALU.mult, inc=("ve" if ch == 7 else None))
    P.wait("tensor", "ve", P.val("ve"))
    for ch in range(8):
        bk = 6 + ch % 2
        P.wait("tensor", f"pe{bk}", P.val(f"pe{bk}"))
        P.tp(c["pb"][bk][0:NE, 0:128], m_tm[:, ch, :], c["ident"][:],
             inc=f"pf{bk}")
        P.wait("scalar", f"pf{bk}", P.val(f"pf{bk}"))
        P.act(c_fm[:, ch * 128:(ch + 1) * 128], c["pb"][bk][0:NE, 0:128],
              AF.Copy, inc=f"pe{bk}")
    P.nop("scalar", inc="se")



def emit_l0_experts(P, c, xr, c_fm, sel8, ws, stq, h3_sb, act, tmp_e, cb,
                    ws1T, ws3T, ws2T, we1T, we3T, we2T, out_hbm):
    pb = c["pb"]
    units = []
    for e in range(-1, NE):
        if e < 0:
            units.append(("h3", ws3T, None))
            units.append(("h1", ws1T, None))
            units.append(("w2", ws2T, None))
        else:
            units.append(("h3", we3T[e], e))
            units.append(("h1", we1T[e], e))
            units.append(("w2", we2T[e], e))
    loaded = []
    li = [0]

    def ensure_loaded(idx):
        while li[0] <= idx:
            kind, hbm, e = units[li[0]]
            nkt = KT if kind in ("h1", "h3") else FT
            m = FF if kind in ("h1", "h3") else D
            loaded.append(ws.load(hbm, nkt, m))
            li[0] += 1

    def rel(evt):
        ws.release(evt)
        nxt = len(ws.rel_evts) + 1
        if nxt < len(units):
            ensure_loaded(nxt)

    ensure_loaded(1)
    for ei in range(9):
        e = ei - 1
        u_h3, u_h1, u_w2 = None, None, None
        cb_evt = []
        if e >= 0:
            for tc in range(2):
                bk = 6 + tc
                P.wait("tensor", f"pe{bk}", P.val(f"pe{bk}"))
                P.mm(pb[bk][:, :], sel8[:, e * 128:(e + 1) * 128],
                     c_fm[0:NE, tc * 512:(tc + 1) * 512], True, True,
                     inc=f"pf{bk}")
                P.wait("scalar", f"pf{bk}", P.val(f"pf{bk}"))
                P.act(cb[:, tc * 512:(tc + 1) * 512], pb[bk][:, :], AF.Copy,
                      inc=f"pe{bk}")
                cb_evt.append((f"pe{bk}", P.val(f"pe{bk}")))
        u_h3 = loaded[3 * ei]
        ws.consume_wait(u_h3)
        for oi in range(FT):
            for tc in range(2):
                g = oi * 2 + tc
                bk = g % 2
                P.wait("tensor", f"pe{bk}", P.val(f"pe{bk}"))
                for kt in range(KT):
                    P.mm(pb[bk][:, :], ws.tile(u_h3, kt, oi * 128, 128),
                         xr[:, kt, tc * 512:(tc + 1) * 512], kt == 0,
                         kt == KT - 1, inc=(f"pf{bk}" if kt == KT - 1 else None))
                P.wait("scalar", f"pf{bk}", P.val(f"pf{bk}"))
                P.act(h3_sb[:, oi, tc * 512:(tc + 1) * 512], pb[bk][:, :],
                      AF.Copy, inc=f"pe{bk}")
        rel((f"pe{(FT * 2 - 1) % 2}", P.val(f"pe{(FT * 2 - 1) % 2}")))
        u_h1 = loaded[3 * ei + 1]
        ws.consume_wait(u_h1)
        for sem_e, val_e in cb_evt:
            P.wait("vector", sem_e, val_e)
        for oi in range(FT):
            for tc in range(2):
                g = oi * 2 + tc
                bk = 2 + g % 2
                P.wait("tensor", f"pe{bk}", P.val(f"pe{bk}"))
                for kt in range(KT):
                    P.mm(pb[bk][:, :], ws.tile(u_h1, kt, oi * 128, 128),
                         xr[:, kt, tc * 512:(tc + 1) * 512], kt == 0,
                         kt == KT - 1, inc=(f"pf{bk}" if kt == KT - 1 else None))
                P.wait("scalar", f"pf{bk}", P.val(f"pf{bk}"))
                P.act(tmp_e[:, g % 2, :], pb[bk][:, :], AF.Silu, inc="se")
                P.wait("vector", "se", P.val("se"))
                tsl = slice(tc * 512, (tc + 1) * 512)
                P.tt("vector", act[:, oi, tsl], tmp_e[:, g % 2, :],
                     h3_sb[:, oi, tsl], ALU.mult,
                     inc=(None if e >= 0 else f"pe{bk}"))
                if e >= 0:
                    P.tt("vector", act[:, oi, tsl], act[:, oi, tsl],
                         cb[:, tsl], ALU.mult, inc=f"pe{bk}")
        act_evt = [("pe2", P.val("pe2")), ("pe3", P.val("pe3"))]
        rel(act_evt[(FT * 2 - 1) % 2])
        u_w2 = loaded[3 * ei + 2]
        ws.consume_wait(u_w2)
        for sem_e, val_e in act_evt:
            P.wait("tensor", sem_e, val_e)
        for sslot in range(4):
            P.wait("gpsimd", f"xa{sslot}", P.val(f"xa{sslot}"))
        for oi in range(KT):
            for tc in range(2):
                g = oi * 2 + tc
                bk = 4 + g % 2
                P.wait("tensor", f"pe{bk}", P.val(f"pe{bk}"))
                for kt in range(FT):
                    P.mm(pb[bk][:, :], ws.tile(u_w2, kt, oi * 128, 128),
                         act[:, kt, tc * 512:(tc + 1) * 512], kt == 0,
                         kt == FT - 1, inc=(f"pf{bk}" if kt == FT - 1 else None))
                slot = stq.uses % stq.n
                stq.pre_evac_wait()
                P.wait("scalar", f"pf{bk}", P.val(f"pf{bk}"))
                P.act(stq.slot(), pb[bk][:, :], AF.Identity, scale=c["half_col"][:],
                      inc=f"pe{bk}")
                dst = out_hbm[oi * 128:(oi + 1) * 128,
                              tc * 512:(tc + 1) * 512]
                s = stq.stg[:, slot, :]
                P.wait("gpsimd", f"pe{bk}", P.val(f"pe{bk}"))
                P.dma("gpsimd", dst, s, sem=f"xa{slot}", accum=True)
                stq.slot_evt[slot] = (f"xa{slot}", P.val(f"xa{slot}"))
                stq.uses += 1
        last_bk = 4 + (KT * 2 - 1) % 2
        rel((f"pe{last_bk}", P.val(f"pe{last_bk}")))



def emit_l0_attn_front(P, c, xr, ws, W_in, brow, bt16, blast, btbias,
                       silu_v, sxc, dB_o, dlt_o, bt_sb, srow, srow2, srow3,
                       stq):
    pb = c["pb"]
    specs = [(W_in[:, u * 512:(u + 1) * 512], KT, 512) for u in range(8)]
    specs.append((W_in[:, 4096:4113], KT, 17))
    units = [ws.load(*specs[0]), ws.load(*specs[1])]

    def rel2(evt):
        ws.release(evt)
        if len(units) < len(specs):
            units.append(ws.load(*specs[len(units)]))

    for u in range(8):
        unit = units[u]
        ws.consume_wait(unit)
        hbm = silu_v if u < 4 else sxc
        base = (u % 4) * 512
        for j in range(4):
            oi = base // 128 + j
            for tc in range(2):
                g = j * 2 + tc
                bk = g % 4
                P.wait("tensor", f"pe{bk}", P.val(f"pe{bk}"))
                for kt in range(KT):
                    P.mm(pb[bk][:, :], ws.tile(unit, kt, j * 128, 128),
                         xr[:, kt, tc * 512:(tc + 1) * 512], kt == 0,
                         kt == KT - 1, inc=(f"pf{bk}" if kt == KT - 1 else None))
                stq.pre_evac_wait()
                P.wait("scalar", f"pf{bk}", P.val(f"pf{bk}"))
                P.act(stq.slot(), pb[bk][:, :], AF.Silu,
                      bias=brow[:, u * 4 + j:u * 4 + j + 1], inc=f"pe{bk}")
                stq.store(hbm[oi * 128:(oi + 1) * 128,
                              tc * 512:(tc + 1) * 512],
                          (f"pe{bk}", P.val(f"pe{bk}")))
        rel2((f"pe{(8 - 1) % 4}", P.val(f"pe{(8 - 1) % 4}")))
    u_tail = units[8]
    ws.consume_wait(u_tail)
    for tc in range(2):
        sl = slice(tc * 512, (tc + 1) * 512)
        P.wait("tensor", "pe4", P.val("pe4"))
        for kt in range(KT):
            P.mm(pb[4][0:16, :], ws.tile(u_tail, kt, 0, 16), xr[:, kt, sl],
                 kt == 0, kt == KT - 1, inc=("pf4" if kt == KT - 1 else None))
        P.wait("scalar", "pf4", P.val("pf4"))
        P.act(bt_sb[:, sl], pb[4][0:16, :], AF.Identity, bias=btbias[:], inc="pe4")
        P.wait("tensor", "pe5", P.val("pe5"))
        for kt in range(KT):
            P.mm(pb[5][0:1, :], ws.tile(u_tail, kt, 16, 1), xr[:, kt, sl],
                 kt == 0, kt == KT - 1, inc=("pf5" if kt == KT - 1 else None))
        P.wait("scalar", "pf5", P.val("pf5"))
        P.act(srow[0:1, sl], pb[5][0:1, :], AF.Identity, bias=blast[:], inc="pe5")
    rel2(("pe5", P.val("pe5")))
    P.act(srow2[0:1, :], srow[0:1, :], AF.Abs)
    P.act(srow2[0:1, :], srow2[0:1, :], AF.Exp, scale=-1.0)
    P.act(srow2[0:1, :], srow2[0:1, :], AF.Ln, bias=1.0, inc="se")
    P.wait("vector", "se", P.val("se"))
    P.ts("vector", srow3[0:1, :], srow[0:1, :], 0.0, ALU.max)
    P.tt("vector", srow3[0:1, :], srow3[0:1, :], srow2[0:1, :], ALU.add,
         inc="ve")
    P.wait("scalar", "ve", P.val("ve"))
    P.act(srow2[0:1, :], srow3[0:1, :], AF.Sigmoid, inc="se")
    P.wait("vector", "se", P.val("se"))
    P.tt("vector", srow[0:1, :], srow3[0:1, :], srow2[0:1, :], ALU.mult,
         inc="ve")
    P.wait("sync", "ve", P.val("ve"))
    P.dma("sync", dlt_o[:], srow[0:1, :], sem="st")
    P.wait("tensor", "ve", P.val("ve"))
    for tc in range(2):
        sl = slice(tc * 512, (tc + 1) * 512)
        bk = 6 + tc
        P.wait("tensor", f"pe{bk}", P.val(f"pe{bk}"))
        P.mm(pb[bk][0:16, :], c["ones_row"][0:1, 0:16], srow[0:1, sl], True,
             True, inc=f"pf{bk}")
        P.wait("vector", f"pf{bk}", P.val(f"pf{bk}"))
        P.tt("vector", bt_sb[:, sl], bt_sb[:, sl], pb[bk][0:16, :], ALU.mult,
             inc=f"pe{bk}")
    P.nop("vector", inc="ve")
    P.wait("sync", "ve", P.val("ve"))
    P.dma("sync", dB_o[:], bt_sb[:], sem="st")



def build_launch1():
    P = Prog()
    nc = P.nc
    xT = nc.dram_tensor("xT", [D, TOK], F32, kind="ExternalInput").ap()
    gate_wT = nc.dram_tensor("gate_wT", [D, NE], F32, kind="ExternalInput").ap()
    bias8 = nc.dram_tensor("bias8", [1, NE], F32, kind="ExternalInput").ap()
    ws1T = nc.dram_tensor("ws1T", [D, FF], F32, kind="ExternalInput").ap()
    ws3T = nc.dram_tensor("ws3T", [D, FF], F32, kind="ExternalInput").ap()
    ws2T = nc.dram_tensor("ws2T", [FF, D], F32, kind="ExternalInput").ap()
    we1T = nc.dram_tensor("we1T", [NE, D, FF], F32, kind="ExternalInput").ap()
    we3T = nc.dram_tensor("we3T", [NE, D, FF], F32, kind="ExternalInput").ap()
    we2T = nc.dram_tensor("we2T", [NE, FF, D], F32, kind="ExternalInput").ap()
    W_in = nc.dram_tensor("W_in", [D, 4113], F32, kind="ExternalInput").ap()
    b_main = nc.dram_tensor("b_main", [128, 32], F32, kind="ExternalInput").ap()
    b_t16 = nc.dram_tensor("b_t16", [16, 1], F32, kind="ExternalInput").ap()
    b_last = nc.dram_tensor("b_last", [1, 1], F32, kind="ExternalInput").ap()
    B_par = nc.dram_tensor("B_par", [16, 1], F32, kind="ExternalInput").ap()

    x1T = nc.dram_tensor("x1T", [D, TOK], F32, kind="ExternalOutput").ap()
    silu_v = nc.dram_tensor("silu_v", [D, TOK], F32, kind="ExternalOutput").ap()
    sxc = nc.dram_tensor("sxc", [D, TOK], F32, kind="ExternalOutput").ap()
    dB_o = nc.dram_tensor("dB_o", [16, TOK], F32, kind="ExternalOutput").ap()
    dlt_o = nc.dram_tensor("dlt_o", [1, TOK], F32, kind="ExternalOutput").ap()

    xr = nc.alloc_sbuf_tensor("xr", [128, KT, TOK], BF16)
    big = nc.alloc_sbuf_tensor("big", [128, 11264], F32)
    xt = big[:, 0:8192].rearrange("p (s a t) -> p s a t", s=2, a=KT)
    h3_sb = big[:, 0:5632].bitcast(BF16).rearrange("p (a t) -> p a t", a=FT)
    act = big[:, 5632:11264].bitcast(BF16).rearrange("p (a t) -> p a t", a=FT)
    mean_r = big[0:1, 8192:9216]
    msq_r = big[0:1, 9216:10240]
    var_r = big[0:1, 10240:11264]
    rstd_r = big[0:1, 9216:10240]
    mr_r = big[0:1, 8192:9216]
    rows = [mean_r, msq_r, var_r, rstd_r, mr_r]
    l_fm = big[0:NE, 0:1024]
    srow = big[0:1, 0:1024]
    srow2 = big[0:1, 1024:2048]
    srow3 = big[0:1, 2048:3072]
    bt_sb = big[0:16, 3072:4096]
    tmp_e = nc.alloc_sbuf_tensor("tmp_e", [128, 2, 512], F32)
    cb = nc.alloc_sbuf_tensor("cb", [128, TOK], F32)
    c_fm = nc.alloc_sbuf_tensor("c_fm", [NE, TOK], F32)
    r_tm = nc.alloc_sbuf_tensor("r_tm", [128, 8, NE], F32)
    s_tm = nc.alloc_sbuf_tensor("s_tm", [128, 8, NE], F32)
    m_tm = nc.alloc_sbuf_tensor("m_tm", [128, 8, NE], F32)
    w_tm = nc.alloc_sbuf_tensor("w_tm", [128, 8, NE], F32)
    r1 = nc.alloc_sbuf_tensor("r1c", [128, 8], F32)
    r2 = nc.alloc_sbuf_tensor("r2c", [128, 8], F32)
    bias_b = nc.alloc_sbuf_tensor("bias_b", [128, NE], F32)
    sel8 = nc.alloc_sbuf_tensor("sel8", [NE, NE * 128], F32)
    brow = nc.alloc_sbuf_tensor("brow", [128, 32], F32)
    bt16 = nc.alloc_sbuf_tensor("bt16", [16, 1], F32)
    blast = nc.alloc_sbuf_tensor("blast", [1, 1], F32)
    bpar = nc.alloc_sbuf_tensor("bpar", [16, 1], F32)
    btbias = nc.alloc_sbuf_tensor("btbias", [16, 1], F32)
    gwT = nc.alloc_sbuf_tensor("gwT", [128, KT, NE], BF16)

    c = alloc_common(P)
    ws = WStream(P, "wb")
    stq = StoreRot(P, "stg", 4, 512)

    P.mset(sel8[:], 1.0, inc="ge")
    P.wait("gpsimd", "ge", P.val("ge"))
    P.asel(sel8[:].rearrange("p (e q) -> p e q", e=NE), 
           sel8[:].rearrange("p (e q) -> p e q", e=NE),
           [[-1, NE], [0, 128]], ALU.is_equal, 0.0, 0, 1, inc="ge")

    P.dma("gpsimd", gwT[:], gate_wT.rearrange("(a p) m -> p a m", p=128),
          sem="ldc")
    P.dma("sync", brow[:], b_main[:], sem="ld")
    P.dma("sync", bt16[:], b_t16[:], sem="ld")
    P.dma("sync", blast[:], b_last[:], sem="ld")
    P.dma("sync", bpar[:], B_par[:], sem="ld")
    P.dma("sync", bias_b[0:1, :], bias8[:], sem="ld")
    P.dma("gpsimd", x1T[:], xT[:], sem="xai")
    P.wait("sync", "ld")
    P.wait("gpsimd", "ldc")
    P.wait("gpsimd", "xai", 16)
    P.barrier()
    P.tt("vector", btbias[:], bt16[:], bpar[:], ALU.add, inc="ve")
    P.barrier()

    emit_ln_stream(P, c, xT, xr, xt, rows, "ln1")
    P.barrier()
    emit_l0_routing(P, c, xr, gwT, bias_b, l_fm, c_fm, r_tm, s_tm, m_tm, w_tm,
                    r1, r2)
    P.barrier()
    emit_l0_experts(P, c, xr, c_fm, sel8, ws, stq, h3_sb, act, tmp_e, cb,
                    ws1T, ws3T, ws2T, we1T, we3T, we2T, x1T)
    for sslot in range(4):
        P.wait("gpsimd", f"xa{sslot}", P.val(f"xa{sslot}"))
    P.barrier()
    emit_ln_stream(P, c, x1T, xr, xt, rows, "ln2")
    P.barrier()
    emit_l0_attn_front(P, c, xr, ws, W_in, brow, bt16, blast, btbias,
                       silu_v, sxc, dB_o, dlt_o, bt_sb, srow, srow2, srow3,
                       stq)
    stq.wait_all("sync")
    P.wait("sync", "st")
    P.barrier()
    return P



def _prep_weights_l1(inp):
    w = {}
    w["gate_wT"] = np.ascontiguousarray(inp["L0_ffn_gate_w"].T)
    w["bias8"] = np.ascontiguousarray(inp["L0_ffn_bias"].reshape(1, NE))
    w["ws1T"] = np.ascontiguousarray(inp["L0_ffn_ws1"].T)
    w["ws3T"] = np.ascontiguousarray(inp["L0_ffn_ws3"].T)
    w["ws2T"] = np.ascontiguousarray(inp["L0_ffn_ws2"].T)
    w["we1T"] = np.ascontiguousarray(inp["L0_ffn_we1"].transpose(0, 2, 1))
    w["we3T"] = np.ascontiguousarray(inp["L0_ffn_we3"].transpose(0, 2, 1))
    w["we2T"] = np.ascontiguousarray(inp["L0_ffn_we2"].transpose(0, 2, 1))
    w["W_in"] = np.ascontiguousarray(inp["L0_attn_W_in"])
    b_in = inp["L0_attn_b_in"]
    w["b_main"] = np.ascontiguousarray(b_in[:4096].reshape(32, 128).T)
    w["b_t16"] = np.ascontiguousarray(b_in[4096:4112].reshape(16, 1))
    w["b_last"] = np.ascontiguousarray(b_in[4112:].reshape(1, 1))
    w["B_par"] = np.ascontiguousarray(inp["L0_attn_B_param"].reshape(16, 1))
    return w


def core_token_slice(xfull, cidx):
    b, h = cidx // 2, cidx % 2
    return np.ascontiguousarray(xfull[b, h * TOK:(h + 1) * TOK, :].T)



def build_launch2():
    P = Prog()
    nc = P.nc
    BS = B * S
    sxc = nc.dram_tensor("sxc", [256, BS], F32, kind="ExternalInput").ap()
    dB = nc.dram_tensor("dB", [16, BS], F32, kind="ExternalInput").ap()
    dlt = nc.dram_tensor("dlt", [1, BS], F32, kind="ExternalInput").ap()
    A_log = nc.dram_tensor("A_log", [256, NST], F32, kind="ExternalInput").ap()
    C_par = nc.dram_tensor("C_par", [256, NST], F32, kind="ExternalInput").ap()
    conv_w = nc.dram_tensor("conv_w", [256, 4], F32, kind="ExternalInput").ap()
    y_o = nc.dram_tensor("y_o", [256, BS], F32, kind="ExternalOutput").ap()

    A_sb = nc.alloc_sbuf_tensor("A_sb", [128, 2, NST], F32)
    C_sb = nc.alloc_sbuf_tensor("C_sb", [128, 2, NST], F32)
    w_sb = nc.alloc_sbuf_tensor("w_sb", [128, 2, 4], F32)
    sxt = nc.alloc_sbuf_tensor("sxt", [128, 2, 2051], F32)
    xconv = nc.alloc_sbuf_tensor("xconv", [128, 2, S], F32)
    dA = nc.alloc_sbuf_tensor("dA", [128, NST, 512], F32)
    u_t = nc.alloc_sbuf_tensor("u_t", [128, NST, 512], F32)
    h_t = nc.alloc_sbuf_tensor("h_t", [128, NST, 512], F32)
    dB128 = nc.alloc_sbuf_tensor("dB128", [128, NST, 512], F32)
    d128 = nc.alloc_sbuf_tensor("d128", [128, 512], F32)
    y_sb = nc.alloc_sbuf_tensor("y_sb", [128, 2, 2, 512], F32)
    dbrow = nc.alloc_sbuf_tensor("dbrow", [1, NST * 512], F32)
    drow = nc.alloc_sbuf_tensor("drow", [1, 512], F32)
    hl = nc.alloc_sbuf_tensor("hl", [128, 2, NST], F32)

    c = alloc_common(P)
    pb = c["pb"]

    P.dma("sync", A_sb[:], A_log.rearrange("(a p) n -> p a n", p=128), sem="ld")
    P.dma("sync", C_sb[:], C_par.rearrange("(a p) n -> p a n", p=128), sem="ld")
    P.dma("sync", w_sb[:], conv_w.rearrange("(a p) n -> p a n", p=128), sem="ld")
    P.wait("sync", "ld")
    P.barrier()
    P.act(A_sb[:].rearrange("p a n -> p (a n)"),
          A_sb[:].rearrange("p a n -> p (a n)"), AF.Exp, inc="se")
    P.wait("vector", "se", P.val("se"))
    P.ts("vector", A_sb[:].rearrange("p a n -> p (a n)"),
         A_sb[:].rearrange("p a n -> p (a n)"), -1.0, ALU.mult, inc="ve")
    P.barrier()

    NCH = S // 512
    P.mset(sxt[:, 0, 0:3], 0.0)
    P.mset(sxt[:, 1, 0:3], 0.0, inc="ge")
    P.wait("sync", "ge", P.val("ge"))
    conv_evt = [None]
    for b in range(B):
        if conv_evt[0] is not None:
            P.wait("sync", conv_evt[0][0], conv_evt[0][1])
        for dt_ in range(2):
            P.dma("sync", sxt[:, dt_, 3:3 + S],
                  sxc[dt_ * 128:(dt_ + 1) * 128, b * S:(b + 1) * S],
                  sem=f"lsx{b % 2}")
        P.wait("vector", f"lsx{b % 2}", P.val(f"lsx{b % 2}"))
        P.wait("vector", "ge", P.val("ge"))
        for dt_ in range(2):
            P.ts("vector", xconv[:, dt_, :], sxt[:, dt_, 3:3 + S],
                 w_sb[:, dt_, 3:4], ALU.mult)
            for k in range(3):
                P.stt("vector", xconv[:, dt_, :], sxt[:, dt_, k:k + S],
                      w_sb[:, dt_, k:k + 1], xconv[:, dt_, :], ALU.mult,
                      ALU.add, inc=("ve" if (dt_ == 1 and k == 2) else None))
        conv_evt[0] = ("ve", P.val("ve"))
        if True:
            pass
        for ch in range(NCH):
            sl = slice(ch * 512, (ch + 1) * 512)
            if not (b == 0 and ch == 0):
                P.wait("sync", "pf7", P.val("pf7"))
            P.dma("sync",
                  dbrow[0:1, :].rearrange("o (n t) -> o n t", n=NST),
                  dB[0:16, b * S + ch * 512:b * S + (ch + 1) * 512],
                  sem="ldb")
            if not (b == 0 and ch == 0):
                P.wait("sync", "pf6", P.val("pf6"))
            P.dma("sync", drow[0:1, :],
                  dlt[0:1, b * S + ch * 512:b * S + (ch + 1) * 512], sem="ldb")
            P.wait("tensor", "ldb", P.val("ldb"))
            P.wait("tensor", f"lsx{b % 2}", P.val(f"lsx{b % 2}"))
            P.wait("tensor", "pe6", P.val("pe6"))
            P.mm(pb[6][:, :], c["ones_row"][:], drow[0:1, :], True, True,
                 inc="pf6")
            P.wait("scalar", "pf6", P.val("pf6"))
            P.act(d128[:, :], pb[6][:, :], AF.Copy, inc="pe6")
            for n in range(NST):
                bk = 6 + n % 2
                P.wait("tensor", f"pe{bk}", P.val(f"pe{bk}"))
                P.mm(pb[bk][:, :], c["ones_row"][:],
                     dbrow[0:1, n * 512:(n + 1) * 512], True, True,
                     inc=f"pf{bk}")
                P.wait("scalar", f"pf{bk}", P.val(f"pf{bk}"))
                P.act(dB128[:, n, :], pb[bk][:, :], AF.Copy, inc=f"pe{bk}")
            P.nop("scalar", inc="se")
            for dt_ in range(2):
                if not (b == 0 and ch == 0 and dt_ == 0):
                    P.wait("scalar", "sc", P.val("sc"))
                for n in range(NST):
                    P.act(dA[:, n, :], d128[:, :], AF.Exp,
                          scale=A_sb[:, dt_, n:n + 1],
                          inc=("se" if n == NST - 1 else None))
                P.wait("vector", "se", P.val("se"))
                xcb = xconv[:, dt_, sl].unsqueeze(1).broadcast_to(
                    [128, NST, 512])
                P.tt("vector", u_t[:], dB128[:], xcb, ALU.mult, inc="ve")
                for n in range(NST):
                    if ch == 0:
                        init = c["zero_col"][:, 0:1]
                    else:
                        init = hl[:, dt_, n:n + 1]
                    P.scan(h_t[:, n, :], dA[:, n, :], u_t[:, n, :], init,
                           inc=("sc" if n == NST - 1 else None))
                P.cp("vector", hl[:, dt_, :],
                     h_t[:, :, 511:512].rearrange("p n o -> p (n o)"),
                     inc="sc")
                ysl = y_sb[:, ch % 2, dt_, :]
                kk = b * NCH + ch
                if kk >= 2:
                    P.wait("vector", f"yst{ch % 2}", 16 * 2 * (kk // 2))
                for n in range(NST):
                    if n == 0:
                        P.ts("vector", ysl, h_t[:, 0, :],
                             C_sb[:, dt_, 0:1], ALU.mult)
                    else:
                        P.stt("vector", ysl, h_t[:, n, :],
                              C_sb[:, dt_, n:n + 1], ysl,
                              ALU.mult, ALU.add,
                              inc=("ve" if n == NST - 1 else None))
            P.wait("sync", "ve", P.val("ve"))
            for dt_ in range(2):
                P.dma("sync",
                      y_o[dt_ * 128:(dt_ + 1) * 128,
                          b * S + ch * 512:b * S + (ch + 1) * 512],
                      y_sb[:, ch % 2, dt_, :], sem=f"yst{ch % 2}")
    P.wait("sync", "yst0", P.val("yst0"))
    P.wait("sync", "yst1", P.val("yst1"))
    P.barrier()
    return P



def emit_l1_routing(P, c, xr, wg1, l_fm, c_fm, r_tm, s_tm, m_tm, w_tm, r1, r2):
    pb = c["pb"]
    for tc in range(2):
        sl = slice(tc * 512, (tc + 1) * 512)
        P.wait("tensor", "pe6", P.val("pe6"))
        for kt in range(KT):
            P.mm(pb[6][0:NE, :], wg1[:, kt, :], xr[:, kt, sl], kt == 0,
                 kt == KT - 1, inc=("pf6" if kt == KT - 1 else None))
        P.wait("scalar", "pf6", P.val("pf6"))
        P.act(l_fm[:, sl], pb[6][0:NE, :], AF.Copy, inc="pe6")
    P.wait("tensor", "se", P.val("se"))
    P.wait("tensor", "pe6", P.val("pe6"))
    for ch in range(8):
        P.tp(pb[6][:, ch * NE:(ch + 1) * NE], l_fm[:, ch * 128:(ch + 1) * 128],
             c["ident"][0:NE, 0:NE], inc=("pf6" if ch == 7 else None))
    P.wait("vector", "pf6", P.val("pf6"))
    P.cp("vector", r_tm[:].rearrange("p a b -> p (a b)"), pb[6][:, 0:64],
         inc="pe6")
    for ch in range(8):
        P.rmax(r1[:, ch:ch + 1], r_tm[:, ch, :],
               inc=("ve" if ch == 7 else None))
    P.wait("vector", "ve", P.val("ve"))
    P.ts("vector", r2[:, 0:8], r1[:, 0:8], -1.0, ALU.mult, inc="ve")
    P.wait("scalar", "ve", P.val("ve"))
    for ch in range(8):
        P.act(s_tm[:, ch, :], r_tm[:, ch, :], AF.Exp, bias=r2[:, ch:ch + 1],
              inc=("se" if ch == 7 else None))
    P.wait("vector", "se", P.val("se"))
    for ch in range(8):
        P.red("vector", r2[:, ch:ch + 1], s_tm[:, ch, :], ALU.add,
              inc=("ve" if ch == 7 else None))
    P.wait("vector", "ve", P.val("ve"))
    P.recip(r2[:, 0:8], r2[:, 0:8], inc="ve")
    P.wait("vector", "ve", P.val("ve"))
    for ch in range(8):
        P.ts("vector", m_tm[:, ch, :], r_tm[:, ch, :], r1[:, ch:ch + 1],
             ALU.is_ge)
        P.ts("vector", w_tm[:, ch, :], s_tm[:, ch, :], r2[:, ch:ch + 1],
             ALU.mult, inc=("ve" if ch == 7 else None))
    P.wait("vector", "ve", P.val("ve"))
    P.tt("vector", m_tm[:].rearrange("p a b -> p (a b)"),
         m_tm[:].rearrange("p a b -> p (a b)"),
         w_tm[:].rearrange("p a b -> p (a b)"), ALU.mult, inc="ve")
    P.wait("tensor", "ve", P.val("ve"))
    for ch in range(8):
        bk = 6 + ch % 2
        P.wait("tensor", f"pe{bk}", P.val(f"pe{bk}"))
        P.tp(c["pb"][bk][0:NE, 0:128], m_tm[:, ch, :], c["ident"][:],
             inc=f"pf{bk}")
        P.wait("scalar", f"pf{bk}", P.val(f"pf{bk}"))
        P.act(c_fm[:, ch * 128:(ch + 1) * 128], c["pb"][bk][0:NE, 0:128],
              AF.Copy, inc=f"pe{bk}")



def emit_l1_experts(P, c, xr, c_fm, sel8, ws, stq, xe, cb, W1, b1sb, out_hbm):
    pb = c["pb"]

    def rel(evt, pend):
        ws.release(evt)
        if pend:
            u = len(ws.rel_evts) + 1
            if u - 2 < len(pend):
                pass
    cap_kt = (ws.bufs[0].shape[1]) // D
    kt_ranges = []
    k0 = 0
    while k0 < KT:
        kt_ranges.append((k0, min(KT, k0 + cap_kt)))
        k0 += cap_kt
    nu = len(kt_ranges)
    specs = []
    for e in range(NE):
        for (a, bnd) in kt_ranges:
            specs.append((W1[e][a * 128:bnd * 128, :], bnd - a, D))
    units = [ws.load(*specs[0]), ws.load(*specs[1])]

    def rel3(evt):
        ws.release(evt)
        if len(units) < len(specs):
            units.append(ws.load(*specs[len(units)]))

    for e in range(NE):
        cb_evt = []
        for tc in range(2):
            bk = 6 + tc
            P.wait("tensor", f"pe{bk}", P.val(f"pe{bk}"))
            P.mm(pb[bk][:, :], sel8[:, e * 128:(e + 1) * 128],
                 c_fm[0:NE, tc * 512:(tc + 1) * 512], True, True, inc=f"pf{bk}")
            P.wait("scalar", f"pf{bk}", P.val(f"pf{bk}"))
            P.act(cb[:, tc * 512:(tc + 1) * 512], pb[bk][:, :], AF.Copy,
                  inc=f"pe{bk}")
            cb_evt.append((f"pe{bk}", P.val(f"pe{bk}")))
        for sem_e, val_e in cb_evt:
            P.wait("vector", sem_e, val_e)
        if e > 0:
            P.wait("vector", "xeok", P.val("xeok"))
        cbb = cb[:].unsqueeze(1).broadcast_to([128, KT, TOK])
        P.tt("vector", xe[:], xr[:], cbb, ALU.mult, inc="ve")
        xe_evt = ("ve", P.val("ve"))
        eus = [units[nu * e + i] for i in range(nu)]
        for uu in eus:
            ws.consume_wait(uu)
        P.wait("tensor", xe_evt[0], xe_evt[1])
        for sslot in range(4):
            P.wait("gpsimd", f"xa{sslot}", P.val(f"xa{sslot}"))
        for oi in range(KT):
            for tc in range(2):
                g = oi * 2 + tc
                bk = 4 + g % 2
                P.wait("tensor", f"pe{bk}", P.val(f"pe{bk}"))
                for kt in range(KT):
                    ui = kt // cap_kt
                    u = eus[ui]
                    P.mm(pb[bk][:, :],
                         ws.tile(u, kt - kt_ranges[ui][0], oi * 128, 128),
                         xe[:, kt, tc * 512:(tc + 1) * 512], kt == 0,
                         kt == KT - 1, inc=(f"pf{bk}" if kt == KT - 1 else None))
                slot = stq.uses % stq.n
                stq.pre_evac_wait()
                P.wait("scalar", f"pf{bk}", P.val(f"pf{bk}"))
                P.act(stq.slot(), pb[bk][:, :], AF.Identity,
                      scale=c["half_col"][:], inc=f"pe{bk}")
                dst = out_hbm[oi * 128:(oi + 1) * 128, tc * 512:(tc + 1) * 512]
                s = stq.stg[:, slot, :]
                P.wait("gpsimd", f"pe{bk}", P.val(f"pe{bk}"))
                P.dma("gpsimd", dst, s, sem=f"xa{slot}", accum=True)
                stq.slot_evt[slot] = (f"xa{slot}", P.val(f"xa{slot}"))
                stq.uses += 1
        last_bk = 4 + (KT * 2 - 1) % 2
        P.wait("vector", f"pe{last_bk}", P.val(f"pe{last_bk}"))
        P.nop("vector", inc="xeok")
        for _ in range(nu):
            rel3((f"pe{last_bk}", P.val(f"pe{last_bk}")))
    for sslot in range(4):
        P.wait("gpsimd", f"xa{sslot}", P.val(f"xa{sslot}"))
    for oi in range(KT):
        for tc in range(2):
            g = oi * 2 + tc
            bk = 4 + g % 2
            P.wait("tensor", f"pe{bk}", P.val(f"pe{bk}"))
            P.mm(pb[bk][:, :], b1sb[:, oi * 128:(oi + 1) * 128],
                 c_fm[0:NE, tc * 512:(tc + 1) * 512], True, True, inc=f"pf{bk}")
            slot = stq.uses % stq.n
            stq.pre_evac_wait()
            P.wait("scalar", f"pf{bk}", P.val(f"pf{bk}"))
            P.act(stq.slot(), pb[bk][:, :], AF.Identity,
                  scale=c["half_col"][:], inc=f"pe{bk}")
            dst = out_hbm[oi * 128:(oi + 1) * 128, tc * 512:(tc + 1) * 512]
            s = stq.stg[:, slot, :]
            P.wait("gpsimd", f"pe{bk}", P.val(f"pe{bk}"))
            P.dma("gpsimd", dst, s, sem=f"xa{slot}", accum=True)
            stq.slot_evt[slot] = (f"xa{slot}", P.val(f"xa{slot}"))
            stq.uses += 1



def emit_l1_proj(P, c, xr, ws, stq, WqT, WkT, WvT, WgT, WbT, WaT, dtb, nA6,
                 q_o, k_o, v_o, sg_o, beta_o, g_o, grow, grow2, grow3):
    pb = c["pb"]
    specs = []
    for u in range(3):
        specs.append(("q", WqT[:, u * 512:(u + 1) * 512], q_o, u * 512, None))
    for u in range(3):
        specs.append(("k", WkT[:, u * 512:(u + 1) * 512], k_o, u * 512, None))
    for u in range(6):
        specs.append(("v", WvT[:, u * 512:(u + 1) * 512], v_o, u * 512, None))
    for u in range(6):
        specs.append(("g", WgT[:, u * 512:(u + 1) * 512], sg_o, u * 512,
                      AF.Silu))
    units = [ws.load(specs[0][1], KT, 512), ws.load(specs[1][1], KT, 512)]

    def rel4(evt):
        ws.release(evt)
        if len(units) < len(specs):
            units.append(ws.load(specs[len(units)][1], KT, 512))

    for ui, (kind, hbm_w, hbm_o, obase, func) in enumerate(specs):
        unit = units[ui]
        ws.consume_wait(unit)
        for j in range(4):
            for tc in range(2):
                g = j * 2 + tc
                bk = g % 4
                P.wait("tensor", f"pe{bk}", P.val(f"pe{bk}"))
                for kt in range(KT):
                    P.mm(pb[bk][:, :], ws.tile(unit, kt, j * 128, 128),
                         xr[:, kt, tc * 512:(tc + 1) * 512], kt == 0,
                         kt == KT - 1, inc=(f"pf{bk}" if kt == KT - 1 else None))
                stq.pre_evac_wait()
                P.wait("scalar", f"pf{bk}", P.val(f"pf{bk}"))
                P.act(stq.slot(), pb[bk][:, :], func or AF.Copy, inc=f"pe{bk}")
                r0 = obase + j * 128
                stq.store(hbm_o[r0:r0 + 128, tc * 512:(tc + 1) * 512],
                          (f"pe{bk}", P.val(f"pe{bk}")))
        rel4((f"pe{3}", P.val("pe3")))
    wba = ws.load(None, 0, 0) if False else None
    ub = ws.load(WbT, KT, 6)
    ua = ws.load(WaT, KT, 6)
    ws.consume_wait(ub)
    ws.consume_wait(ua)
    for tc in range(2):
        sl = slice(tc * 512, (tc + 1) * 512)
        P.wait("tensor", "pe4", P.val("pe4"))
        for kt in range(KT):
            P.mm(pb[4][0:6, :], ws.tile(ub, kt, 0, 6), xr[:, kt, sl],
                 kt == 0, kt == KT - 1, inc=("pf4" if kt == KT - 1 else None))
        P.wait("scalar", "pf4", P.val("pf4"))
        P.act(grow[0:6, sl], pb[4][0:6, :], AF.Sigmoid, inc="pe4")
        P.wait("tensor", "pe5", P.val("pe5"))
        for kt in range(KT):
            P.mm(pb[5][0:6, :], ws.tile(ua, kt, 0, 6), xr[:, kt, sl],
                 kt == 0, kt == KT - 1, inc=("pf5" if kt == KT - 1 else None))
        P.wait("scalar", "pf5", P.val("pf5"))
        P.act(grow2[0:6, sl], pb[5][0:6, :], AF.Identity, bias=dtb[:],
              inc="pe5")
    P.nop("scalar", inc="se")
    P.wait("sync", "se", P.val("se"))
    P.dma("sync", beta_o[:], grow[0:6, :], sem="st")
    P.act(grow3[0:6, :], grow2[0:6, :], AF.Abs)
    P.act(grow3[0:6, :], grow3[0:6, :], AF.Exp, scale=-1.0)
    P.act(grow3[0:6, :], grow3[0:6, :], AF.Ln, bias=1.0, inc="se")
    P.wait("vector", "se", P.val("se"))
    P.ts("vector", grow2[0:6, :], grow2[0:6, :], 0.0, ALU.max)
    P.tt("vector", grow2[0:6, :], grow2[0:6, :], grow3[0:6, :], ALU.add)
    P.ts("vector", grow2[0:6, :], grow2[0:6, :], nA6[:], ALU.mult, inc="ve")
    P.wait("sync", "ve", P.val("ve"))
    P.dma("sync", g_o[:], grow2[0:6, :], sem="st")
    rel4(("pe4", P.val("pe4")))
    rel4(("pe5", P.val("pe5")))



def build_launch3():
    P = Prog()
    nc = P.nc
    x1T = nc.dram_tensor("x1T", [D, TOK], F32, kind="ExternalInput").ap()
    yT = nc.dram_tensor("yT", [D, TOK], F32, kind="ExternalInput").ap()
    svT = nc.dram_tensor("svT", [D, TOK], F32, kind="ExternalInput").ap()
    gate_wT = nc.dram_tensor("gate_wT", [D, NE], F32, kind="ExternalInput").ap()
    bias8 = nc.dram_tensor("bias8", [1, NE], F32, kind="ExternalInput").ap()
    ws1T = nc.dram_tensor("ws1T", [D, FF], F32, kind="ExternalInput").ap()
    ws3T = nc.dram_tensor("ws3T", [D, FF], F32, kind="ExternalInput").ap()
    ws2T = nc.dram_tensor("ws2T", [FF, D], F32, kind="ExternalInput").ap()
    we1T = nc.dram_tensor("we1T", [NE, D, FF], F32, kind="ExternalInput").ap()
    we3T = nc.dram_tensor("we3T", [NE, D, FF], F32, kind="ExternalInput").ap()
    we2T = nc.dram_tensor("we2T", [NE, FF, D], F32, kind="ExternalInput").ap()
    Wg1 = nc.dram_tensor("Wg1", [D, NE], F32, kind="ExternalInput").ap()
    W1 = nc.dram_tensor("W1", [NE, D, D], F32, kind="ExternalInput").ap()
    b1 = nc.dram_tensor("b1", [NE, D], F32, kind="ExternalInput").ap()
    WqT = nc.dram_tensor("WqT", [D, 1536], F32, kind="ExternalInput").ap()
    WkT = nc.dram_tensor("WkT", [D, 1536], F32, kind="ExternalInput").ap()
    WvT = nc.dram_tensor("WvT", [D, 3072], F32, kind="ExternalInput").ap()
    WgT = nc.dram_tensor("WgT", [D, 3072], F32, kind="ExternalInput").ap()
    WbT = nc.dram_tensor("WbT", [D, 6], F32, kind="ExternalInput").ap()
    WaT = nc.dram_tensor("WaT", [D, 6], F32, kind="ExternalInput").ap()
    dtb_i = nc.dram_tensor("dtb_i", [6, 1], F32, kind="ExternalInput").ap()
    A6_i = nc.dram_tensor("A6_i", [6, 1], F32, kind="ExternalInput").ap()

    x4T = nc.dram_tensor("x4T", [D, TOK], F32, kind="ExternalOutput").ap()
    q_o = nc.dram_tensor("q_o", [1536, TOK], F32, kind="ExternalOutput").ap()
    k_o = nc.dram_tensor("k_o", [1536, TOK], F32, kind="ExternalOutput").ap()
    v_o = nc.dram_tensor("v_o", [3072, TOK], F32, kind="ExternalOutput").ap()
    sg_o = nc.dram_tensor("sg_o", [3072, TOK], F32, kind="ExternalOutput").ap()
    beta_o = nc.dram_tensor("beta_o", [6, TOK], F32, kind="ExternalOutput").ap()
    g_o = nc.dram_tensor("g_o", [6, TOK], F32, kind="ExternalOutput").ap()

    x2T = nc.dram_tensor("x2T", [D, TOK], F32).ap()
    x3T = nc.dram_tensor("x3T", [D, TOK], F32).ap()

    xr = nc.alloc_sbuf_tensor("xr", [128, KT, TOK], BF16)
    big = nc.alloc_sbuf_tensor("big", [128, 11264], F32)
    xt = big[:, 0:8192].rearrange("p (s a t) -> p s a t", s=2, a=KT)
    h3_sb = big[:, 0:5632].bitcast(BF16).rearrange("p (a t) -> p a t", a=FT)
    act = big[:, 5632:11264].bitcast(BF16).rearrange("p (a t) -> p a t", a=FT)
    xe = big[:, 0:8192].bitcast(BF16).rearrange("p (a t) -> p a t", a=KT)
    mean_r = big[0:1, 8192:9216]
    msq_r = big[0:1, 9216:10240]
    var_r = big[0:1, 10240:11264]
    rstd_r = big[0:1, 9216:10240]
    mr_r = big[0:1, 8192:9216]
    rows = [mean_r, msq_r, var_r, rstd_r, mr_r]
    l_fm = big[0:NE, 0:1024]
    yt = big[:, 0:4096].rearrange("p (a t) -> p a t", a=KT)
    svt = big[:, 4096:8192].rearrange("p (a t) -> p a t", a=KT)
    grow = big[0:6, 0:1024]
    grow2 = big[0:6, 1024:2048]
    grow3 = big[0:6, 2048:3072]
    tmp_e = nc.alloc_sbuf_tensor("tmp_e", [128, 2, 512], F32)
    cb = nc.alloc_sbuf_tensor("cb", [128, TOK], F32)
    c_fm = nc.alloc_sbuf_tensor("c_fm", [NE, TOK], F32)
    r_tm = nc.alloc_sbuf_tensor("r_tm", [128, 8, NE], F32)
    s_tm = nc.alloc_sbuf_tensor("s_tm", [128, 8, NE], F32)
    m_tm = nc.alloc_sbuf_tensor("m_tm", [128, 8, NE], F32)
    w_tm = nc.alloc_sbuf_tensor("w_tm", [128, 8, NE], F32)
    r1 = nc.alloc_sbuf_tensor("r1c", [128, 8], F32)
    r2 = nc.alloc_sbuf_tensor("r2c", [128, 8], F32)
    bias_b = nc.alloc_sbuf_tensor("bias_b", [128, NE], F32)
    sel8 = nc.alloc_sbuf_tensor("sel8", [NE, NE * 128], F32)
    gwT = nc.alloc_sbuf_tensor("gwT", [128, KT, NE], BF16)
    wg1 = nc.alloc_sbuf_tensor("wg1", [128, KT, NE], BF16)
    b1sb = nc.alloc_sbuf_tensor("b1sb", [NE, D], F32)
    dtb = nc.alloc_sbuf_tensor("dtb", [6, 1], F32)
    nA6 = nc.alloc_sbuf_tensor("nA6", [6, 1], F32)

    c = alloc_common(P)
    ws = WStream(P, "wb")
    stq = StoreRot(P, "stg", 4, 512)

    P.mset(sel8[:], 1.0, inc="ge")
    P.wait("gpsimd", "ge", P.val("ge"))
    P.asel(sel8[:].rearrange("p (e q) -> p e q", e=NE),
           sel8[:].rearrange("p (e q) -> p e q", e=NE),
           [[-1, NE], [0, 128]], ALU.is_equal, 0.0, 0, 1, inc="ge")

    P.dma("gpsimd", gwT[:], gate_wT.rearrange("(a p) m -> p a m", p=128),
          sem="ldc")
    P.dma("gpsimd", wg1[:], Wg1.rearrange("(a p) m -> p a m", p=128),
          sem="ldc")
    P.dma("sync", b1sb[:], b1[:, :], sem="ld")
    P.dma("sync", bias_b[0:1, :], bias8[:], sem="ld")
    P.dma("sync", dtb[:], dtb_i[:], sem="ld")
    P.dma("sync", nA6[:], A6_i[:], sem="ld")
    P.dma("gpsimd", x2T[:], x1T[:], sem="xai")
    P.wait("sync", "ld")
    P.wait("gpsimd", "ldc")
    P.wait("gpsimd", "xai", 16)
    P.barrier()
    P.act(nA6[:], nA6[:], AF.Exp, inc="se")
    P.wait("vector", "se", P.val("se"))
    P.ts("vector", nA6[:], nA6[:], -1.0, ALU.mult, inc="ve")
    P.barrier()

    for i in range(4):
        sl = slice(i * 256, (i + 1) * 256)
        P.dma("sync", yt[:, :, :],
              yT.rearrange("(a p) t -> p a t", p=128)[:, :, sl], sem="lyv")
        P.dma("sync", svt[:, :, :],
              svT.rearrange("(a p) t -> p a t", p=128)[:, :, sl], sem="lyv")
        P.wait("vector", "lyv", P.val("lyv"))
        P.tt("vector", yt[:, :, :], yt[:, :, :], svt[:, :, :], ALU.mult,
             inc="ve")
        P.wait("gpsimd", "ve", P.val("ve"))
        P.wait("gpsimd", "xai", P.val("xai"))
        P.dma("gpsimd", x2T.rearrange("(a p) t -> p a t", p=128)[:, :, sl],
              yt[:, :, :], sem="xai", accum=True)
        P.wait("sync", "xai", P.val("xai"))
    P.barrier()

    P.dma("gpsimd", x3T[:], x2T[:], sem="xai")
    P.wait("gpsimd", "xai")
    emit_ln_stream(P, c, x2T, xr, xt, rows, "ln3")
    P.barrier()
    emit_l0_routing(P, c, xr, gwT, bias_b, l_fm, c_fm, r_tm, s_tm, m_tm, w_tm,
                    r1, r2)
    P.barrier()
    emit_l0_experts(P, c, xr, c_fm, sel8, ws, stq, h3_sb, act, tmp_e, cb,
                    ws1T, ws3T, ws2T, we1T, we3T, we2T, x3T)
    for sslot in range(4):
        P.wait("gpsimd", f"xa{sslot}", P.val(f"xa{sslot}"))
    P.barrier()

    P.dma("gpsimd", x4T[:], x3T[:], sem="xai")
    P.wait("gpsimd", "xai")
    emit_ln_stream(P, c, x3T, xr, xt, rows, "ln4")
    P.barrier()
    emit_l1_routing(P, c, xr, wg1, l_fm, c_fm, r_tm, s_tm, m_tm, w_tm, r1, r2)
    P.barrier()
    emit_l1_experts(P, c, xr, c_fm, sel8, ws, stq, xe, cb, W1, b1sb, x4T)
    for sslot in range(4):
        P.wait("gpsimd", f"xa{sslot}", P.val(f"xa{sslot}"))
    P.barrier()

    emit_ln_stream(P, c, x4T, xr, xt, rows, "ln5")
    P.barrier()
    emit_l1_proj(P, c, xr, ws, stq, WqT, WkT, WvT, WgT, WbT, WaT, dtb, nA6,
                 q_o, k_o, v_o, sg_o, beta_o, g_o, grow, grow2, grow3)
    stq.wait_all("sync")
    P.wait("sync", "st")
    P.barrier()
    return P



def build_launch4(upto='full'):
    P = Prog()
    nc = P.nc
    NH = 3
    CH = S // 128
    qraw = nc.dram_tensor("qraw", [768, S], F32, kind="ExternalInput").ap()
    kraw = nc.dram_tensor("kraw", [768, S], F32, kind="ExternalInput").ap()
    vraw = nc.dram_tensor("vraw", [1536, S], F32, kind="ExternalInput").ap()
    cwq = nc.dram_tensor("cwq", [768, 4], F32, kind="ExternalInput").ap()
    cwk = nc.dram_tensor("cwk", [768, 4], F32, kind="ExternalInput").ap()
    cwv = nc.dram_tensor("cwv", [1536, 4], F32, kind="ExternalInput").ap()
    beta_i = nc.dram_tensor("beta_i", [3, S], F32, kind="ExternalInput").ap()
    g_i = nc.dram_tensor("g_i", [3, S], F32, kind="ExternalInput").ap()
    o_o = nc.dram_tensor("o_o", [S, 1536], F32, kind="ExternalOutput").ap()

    q_fm = nc.alloc_sbuf_tensor("q_fm", [128, 6, S], BF16)
    k_fm = nc.alloc_sbuf_tensor("k_fm", [128, 6, S], BF16)
    k_tm = nc.alloc_sbuf_tensor("k_tm", [128, CH, NH, 256], BF16)
    v_tm = nc.alloc_sbuf_tensor("v_tm", [128, CH, NH, 512], BF16)
    cpad = nc.alloc_sbuf_tensor("cpad", [128, 2, 2051], F32)
    cw_sb = nc.alloc_sbuf_tensor("cw_sb", [128, 24, 4], F32)
    conv_o = nc.alloc_sbuf_tensor("conv_o", [128, 2, S], F32)
    g_sb = nc.alloc_sbuf_tensor("g_sb", [65, S], F32)
    b_sb = nc.alloc_sbuf_tensor("b_sb", [65, S], F32)
    G_sb = g_sb
    s_st = nc.alloc_sbuf_tensor("s_st", [128, NH, 2, 512], F32)
    s_bf = nc.alloc_sbuf_tensor("s_bf", [128, 2, 512], BF16)
    rstdb = nc.alloc_sbuf_tensor("rstdb", [128, 512], F32)
    eps6 = nc.alloc_sbuf_tensor("eps6", [1, 1], F32)
    qsc = nc.alloc_sbuf_tensor("qsc", [1, 1], F32)
    Gb_sb = nc.alloc_sbuf_tensor("Gb_sb", [128, 128], F32)
    Gcol = nc.alloc_sbuf_tensor("Gcol", [128, 1], F32)
    lrow = nc.alloc_sbuf_tensor("lrow", [1, 128], F32)
    lam_b = nc.alloc_sbuf_tensor("lam_b", [128, 128], F32)
    ngp = nc.alloc_sbuf_tensor("ngp", [1, 1], F32)
    dif = nc.alloc_sbuf_tensor("dif", [128, 128], F32)
    dift = nc.alloc_sbuf_tensor("dift", [128, 128], F32)
    ds_ = nc.alloc_sbuf_tensor("ds_", [128, 128], F32)
    dst_ = nc.alloc_sbuf_tensor("dst_", [128, 128], F32)
    dit_ = nc.alloc_sbuf_tensor("dit_", [128, 128], F32)
    bcol = nc.alloc_sbuf_tensor("bcol", [128, 1], F32)
    bb_ = nc.alloc_sbuf_tensor("bb_", [128, 128], F32)
    fcol = nc.alloc_sbuf_tensor("fcol", [128, 1], F32)
    nlcol = nc.alloc_sbuf_tensor("nlcol", [128, 1], F32)
    lccol = nc.alloc_sbuf_tensor("lccol", [128, 1], F32)
    Apow = nc.alloc_sbuf_tensor("Apow", [128, 7, 128], BF16)
    ATpow = nc.alloc_sbuf_tensor("ATpow", [128, 7, 128], BF16)
    nAT = nc.alloc_sbuf_tensor("nAT", [128, 128], BF16)
    identb = nc.alloc_sbuf_tensor("identb", [128, 128], BF16)
    mt_sb = nc.alloc_sbuf_tensor("mt_sb", [128, 128], BF16)
    u_sb = nc.alloc_sbuf_tensor("u_sb", [128, 512], BF16)
    w_sb2 = nc.alloc_sbuf_tensor("w_sb2", [128, 512], F32)
    ks0 = nc.alloc_sbuf_tensor("ks0", [128, 512], F32)
    ktil = nc.alloc_sbuf_tensor("ktil", [128, 256], BF16)
    qlam = nc.alloc_sbuf_tensor("qlam", [128, 2, 128], BF16)
    grow0 = nc.alloc_sbuf_tensor("grow0", [1, 128], F32)
    brow0 = nc.alloc_sbuf_tensor("brow0", [1, 128], F32)

    c = alloc_common(P)
    pb = c["pb"]
    sq2 = cpad[:, :, 0:2048]
    ost = conv_o[:, 0, 0:1024].rearrange("p (s t) -> p s t", s=2)
    rsb = conv_o[0:1, 1, 0:2048]

    P.mset(g_sb[:], 0.0)
    P.mset(b_sb[:], 0.0)
    P.mset(eps6[:], 1e-6)
    P.mset(qsc[:], -2.772588722239781)
    P.mset(s_st[:].rearrange("p a b t -> p (a b t)"), 0.0, inc="ge")
    P.wait("gpsimd", "ge", P.val("ge"))

    P.wait("sync", "ge", P.val("ge"))
    P.dma("sync", cw_sb[:, 0:6, :], cwq.rearrange("(a p) n -> p a n", p=128),
          sem="ld")
    P.dma("sync", cw_sb[:, 6:12, :], cwk.rearrange("(a p) n -> p a n", p=128),
          sem="ld")
    P.dma("sync", cw_sb[:, 12:24, :], cwv.rearrange("(a p) n -> p a n", p=128),
          sem="ld")
    for hh in range(3):
        P.dma("sync", g_sb[32 * hh:32 * hh + 1, :], g_i[hh:hh + 1, :], sem="ld")
        P.dma("sync", b_sb[32 * hh:32 * hh + 1, :], beta_i[hh:hh + 1, :],
              sem="ld")
    P.wait("sync", "ld")
    P.barrier()
    P.cp("vector", identb[:], c["ident"][:], inc="ve")
    P.mset(conv_o[0:65, 0, :], 1.0, inc="ge")
    P.wait("vector", "ge", P.val("ge"))
    P.scan(G_sb[:], conv_o[0:65, 0, :], g_sb[:], c["zero_col"][0:65, :],
           inc="ve")
    P.barrier()

    def conv_block(raw_hbm, cw_off, npt, dst_fn, tag):
        slot_done = [None, None]
        tap_done = [None, None]
        for pt in range(npt):
            sl2 = pt % 2
            if tap_done[sl2] is not None:
                P.wait("sync", tap_done[sl2][0], tap_done[sl2][1])
            P.dma("sync", cpad[:, sl2, 3:3 + S],
                  raw_hbm[pt * 128:(pt + 1) * 128, :], sem=f"cl{sl2}_{tag}")
            P.wait("vector", f"cl{sl2}_{tag}", 16 * (pt // 2 + 1))
            if slot_done[sl2] is not None:
                P.wait("vector", slot_done[sl2][0], slot_done[sl2][1])
            wcol = cw_sb[:, cw_off + pt, :]
            P.ts("vector", conv_o[:, sl2, :], cpad[:, sl2, 3:3 + S],
                 wcol[:, 3:4], ALU.mult)
            for k in range(3):
                P.stt("vector", conv_o[:, sl2, :], cpad[:, sl2, k:k + S],
                      wcol[:, k:k + 1], conv_o[:, sl2, :], ALU.mult, ALU.add,
                      inc=("ve" if k == 2 else None))
            tap_done[sl2] = ("ve", P.val("ve"))
            P.wait("scalar", "ve", P.val("ve"))
            slot_done[sl2] = dst_fn(pt, sl2)

    def q_dst(pt, sl2):
        P.act(conv_o[:, sl2, :], conv_o[:, sl2, :], AF.Silu, inc="se")
        P.wait("vector", "se", P.val("se"))
        P.tt("vector", sq2[:, sl2, :], conv_o[:, sl2, :], conv_o[:, sl2, :],
             ALU.mult, inc="ve")
        P.wait("tensor", "ve", P.val("ve"))
        first = (pt % 2 == 0)
        for tc4 in range(4):
            P.mm(pb[6][0:1, :], c["ones_col"][:],
                 sq2[:, sl2, tc4 * 512:(tc4 + 1) * 512], first, pt % 2 == 1,
                 inc=("pf6" if (pt % 2 == 1 and tc4 == 3) else None))
        P.cp("vector", q_fm[:, pt, :], conv_o[:, sl2, :], inc="ve")
        if pt % 2 == 1:
            hh = pt // 2
            P.wait("scalar", "pf6", P.val("pf6"))
            P.act(rsb[0:1, 0:S], pb[6][0:1, 0:S][0:1, 0:512], AF.Copy) if False else None
            P.q["scalar"].pop()
            P.act(rsb[0:1, 0:2048], pb[6][0:1, 0:512].broadcast_to([1, 512]), AF.Copy) if False else None
            P.q["scalar"].pop()
    conv_block(qraw, 0, 6, q_dst, "q") if False else None

    def mk_dst(dst, pt_off=0, cast_only=True):
        def f(pt, sl2):
            P.act(dst[:, pt + pt_off, :], conv_o[:, sl2, :], AF.Silu, inc="se")
            return ("se", P.val("se"))
        return f

    P.mset(cpad[:, 0, 0:3], 0.0)
    P.mset(cpad[:, 1, 0:3], 0.0, inc="ge")
    P.wait("vector", "ge", P.val("ge"))
    P.wait("sync", "ge", P.val("ge"))
    conv_block(qraw, 0, 6, mk_dst(q_fm), "q")
    P.barrier()
    if upto == 'convq':
        return P
    conv_block(kraw, 6, 6, mk_dst(k_fm), "k")
    P.barrier()
    if upto == 'convqk':
        return P

    vch = [0]
    def v_dst(pt, sl2):
        P.act(conv_o[:, sl2, :], conv_o[:, sl2, :], AF.Silu, inc="se")
        hh = pt // 4
        dvt = pt % 4
        P.wait("tensor", "se", P.val("se"))
        last_pf = None
        for ci in range(CH):
            bk = 6 + ci % 2
            P.wait("tensor", f"pe{bk}", P.val(f"pe{bk}"))
            P.tp(pb[bk][:, 0:128], conv_o[:, sl2, ci * 128:(ci + 1) * 128],
                 c["ident"][:], inc=f"pf{bk}")
            last_pf = (f"pf{bk}", P.val(f"pf{bk}"))
            P.wait("vector", f"pf{bk}", P.val(f"pf{bk}"))
            P.cp("vector", v_tm[:, ci, hh, dvt * 128:(dvt + 1) * 128],
                 pb[bk][:, 0:128], inc=f"pe{bk}")
        return last_pf

    conv_block(vraw, 12, 12, v_dst, "v")
    P.barrier()
    if upto == 'conv':
        return P

    ssq_evt = [None]
    rsb_evt = [None]
    for (fm, isq) in ((q_fm, True), (k_fm, False)):
        for hh in range(NH):
            if ssq_evt[0] is not None:
                P.wait("scalar", ssq_evt[0][0], ssq_evt[0][1])
            if rsb_evt[0] is not None:
                P.wait("scalar", rsb_evt[0][0], rsb_evt[0][1])
            for pt2 in range(2):
                P.act(sq2[:, pt2, :], fm[:, 2 * hh + pt2, :], AF.Square,
                      inc="se")
            P.wait("tensor", "se", P.val("se"))
            for tc4 in range(4):
                bk = 6 + tc4 % 2
                P.wait("tensor", f"pe{bk}", P.val(f"pe{bk}"))
                for pt2 in range(2):
                    P.mm(pb[bk][0:1, :], c["ones_col"][:],
                         sq2[:, pt2, tc4 * 512:(tc4 + 1) * 512], pt2 == 0,
                         pt2 == 1, inc=(f"pf{bk}" if pt2 == 1 else None))
                if tc4 == 3:
                    ssq_evt[0] = (f"pf{bk}", P.val(f"pf{bk}"))
                P.wait("scalar", f"pf{bk}", P.val(f"pf{bk}"))
                P.act(rsb[0:1, tc4 * 512:(tc4 + 1) * 512], pb[bk][0:1, :],
                      AF.Ln, bias=eps6[:], inc=f"pe{bk}")
            P.nop("scalar", inc="se")
            P.wait("scalar", "se", P.val("se"))
            if isq:
                P.act(rsb[0:1, :], rsb[0:1, :], AF.Exp, scale=c["m05"][:],
                      bias=qsc[:], inc="se")
            else:
                P.act(rsb[0:1, :], rsb[0:1, :], AF.Exp, scale=c["m05"][:],
                      inc="se")
            P.wait("tensor", "se", P.val("se"))
            for tc4 in range(4):
                bk = 6 + tc4 % 2
                P.wait("tensor", f"pe{bk}", P.val(f"pe{bk}"))
                P.mm(pb[bk][:, :], c["ones_row"][:],
                     rsb[0:1, tc4 * 512:(tc4 + 1) * 512], True, True,
                     inc=f"pf{bk}")
                if tc4 == 3:
                    rsb_evt[0] = (f"pf{bk}", P.val(f"pf{bk}"))
                P.wait("scalar", f"pf{bk}", P.val(f"pf{bk}"))
                P.act(rstdb[:, :], pb[bk][:, :], AF.Copy, inc=f"pe{bk}")
                P.nop("scalar", inc="se")
                P.wait("vector", "se", P.val("se"))
                for pt2 in range(2):
                    P.tt("vector", fm[:, 2 * hh + pt2,
                                      tc4 * 512:(tc4 + 1) * 512],
                         fm[:, 2 * hh + pt2, tc4 * 512:(tc4 + 1) * 512],
                         rstdb[:, :], ALU.mult, inc="ve")
                P.wait("scalar", "ve", P.val("ve"))
    P.barrier()
    for hh in range(NH):
        for dkt in range(2):
            for ci in range(CH):
                bk = 6 + ci % 2
                P.wait("tensor", f"pe{bk}", P.val(f"pe{bk}"))
                P.tp(pb[bk][:].bitcast(BF16)[:, 0:128],
                     k_fm[:, 2 * hh + dkt, ci * 128:(ci + 1) * 128],
                     identb[:], inc=f"pf{bk}")
                P.wait("vector", f"pf{bk}", P.val(f"pf{bk}"))
                P.cp("vector", k_tm[:, ci, hh, dkt * 128:(dkt + 1) * 128],
                     pb[bk][:].bitcast(BF16)[:, 0:128], inc=f"pe{bk}")
    P.nop("vector", inc="vtm")
    P.barrier()

    m_s = cpad[:, 0, 0:128]
    m_st = cpad[:, 0, 128:256]
    m_it = cpad[:, 0, 256:384]
    P.mset(cpad[:, 0, 0:384], 0.0, inc="ge")
    P.wait("gpsimd", "ge", P.val("ge"))
    P.asel(m_s, m_s, [[-1, 128]], ALU.is_gt, -1e30, 0, 1)
    P.asel(m_st, m_st, [[1, 128]], ALU.is_gt, -1e30, 0, -1)
    P.asel(m_it, m_it, [[1, 128]], ALU.is_gt, -1e30, 1, -1, inc="ge")
    P.wait("vector", "ge", P.val("ge"))
    P.barrier()
    if upto == 'l2norm':
        return P

    it_count = [0]
    for ci in range(CH):
        for hh in range(NH):
            if isinstance(upto, tuple) and upto[0] == 'delta' and \
               it_count[0] >= upto[1]:
                continue
            it_count[0] += 1
            tsl = slice(ci * 128, (ci + 1) * 128)
            P.wait("scalar", "gr0", P.val("gr0"))
            P.act(grow0[0:1, :], G_sb[32 * hh:32 * hh + 1, tsl], AF.Copy)
            P.act(brow0[0:1, :], b_sb[32 * hh:32 * hh + 1, tsl], AF.Copy,
                  inc="se")
            Grow = grow0[0:1, :]
            if ci == 0:
                P.mset(ngp[:], 0.0, inc="ge")
                P.wait("vector", "ge", P.val("ge"))
            else:
                P.ts("vector", ngp[:], G_sb[32 * hh:32 * hh + 1,
                                            ci * 128 - 1:ci * 128], -1.0,
                     ALU.mult)
            P.wait("tensor", "se", P.val("se"))
            P.wait("tensor", "pe6", P.val("pe6"))
            P.mm(pb[6][:, 0:128], c["ones_row"][:], Grow, True, True,
                 inc="pf6")
            P.wait("tensor", "pe7", P.val("pe7"))
            P.tp(pb[7][:, 0:1], Grow, c["ident"][0:1, 0:1], inc="pf7")
            P.wait("scalar", "pf6", P.val("pf6"))
            P.act(Gb_sb[:], pb[6][:, 0:128], AF.Copy, inc="pe6")
            P.wait("scalar", "pf7", P.val("pf7"))
            P.act(Gcol[:], pb[7][:, 0:1], AF.Copy, inc="pe7")
            P.nop("scalar", inc="se")
            P.wait_eng("scalar", "vector")
            P.wait_eng("scalar", "gpsimd")
            P.act(lrow[0:1, :], Grow, AF.Exp, bias=ngp[:], inc="se")
            P.wait("vector", "se", P.val("se"))
            P.ts("vector", dift[:], Gb_sb[:], Gcol[:], ALU.subtract)
            P.ts("vector", dif[:], dift[:], -1.0, ALU.mult, inc="ve")
            P.tt("vector", ds_[:], dif[:], m_s, ALU.add)
            P.tt("vector", dst_[:], dift[:], m_st, ALU.add)
            P.tt("vector", dit_[:], dift[:], m_it, ALU.add, inc="ve")
            P.wait("scalar", "ve", P.val("ve"))
            P.act(ds_[:], ds_[:], AF.Exp)
            P.act(dst_[:], dst_[:], AF.Exp)
            P.act(dit_[:], dit_[:], AF.Exp)
            P.act(fcol[:], dift[:, 127:128], AF.Exp, inc="se")
            P.wait("tensor", "pe6", P.val("pe6"))
            P.mm(pb[6][:, 0:128], c["ones_row"][:],
                 brow0[0:1, :], True, True, inc="pf6")
            P.wait("scalar", "pf6", P.val("pf6"))
            P.act(bb_[:], pb[6][:, 0:128], AF.Copy, inc="pe6")
            P.wait("tensor", "pe7", P.val("pe7"))
            P.tp(pb[7][:, 0:1], brow0[0:1, :],
                 c["ident"][0:1, 0:1], inc="pf7")
            P.wait("scalar", "pf7", P.val("pf7"))
            P.act(bcol[:], pb[7][:, 0:1], AF.Copy, inc="pe7")
            P.wait("tensor", "se", P.val("se"))
            P.wait("tensor", "pe5", P.val("pe5"))
            P.mm(pb[5][:, 0:128], c["ones_row"][:], lrow[0:1, :], True, True,
                 inc="pf5")
            P.wait("scalar", "pf5", P.val("pf5"))
            P.act(lam_b[:], pb[5][:, 0:128], AF.Copy, inc="pe5")
            k1 = k_fm[:, 2 * hh, tsl]
            k2 = k_fm[:, 2 * hh + 1, tsl]
            q1 = q_fm[:, 2 * hh, tsl]
            q2 = q_fm[:, 2 * hh + 1, tsl]
            P.wait("tensor", "pe0", P.val("pe0"))
            P.mm(pb[0][:, 0:128], k1, k1, True, False)
            P.mm(pb[0][:, 0:128], k2, k2, False, True, inc="pf0")
            P.wait("tensor", "pe1", P.val("pe1"))
            P.mm(pb[1][:, 0:128], k1, q1, True, False)
            P.mm(pb[1][:, 0:128], k2, q2, False, True, inc="pf1")
            P.wait("vector", "pf0", P.val("pf0"))
            P.wait_eng("vector", "scalar")
            P.tt("vector", dif[:], pb[0][:, 0:128], ds_[:], ALU.mult)
            P.tt("vector", Apow[:, 0, :], dif[:], bb_[:], ALU.mult)
            P.ts("vector", dift[:], pb[0][:, 0:128], bcol[:], ALU.mult)
            P.tt("vector", ATpow[:, 0, :], dift[:], dst_[:], ALU.mult,
                 inc="pe0")
            P.ts("vector", ATpow[:, 0, :].bitcast(F32)[:, 0:64] if False else nAT[:],
                 ATpow[:, 0, :], -1.0, ALU.mult)
            P.wait("vector", "pf1", P.val("pf1"))
            P.ts("vector", dift[:], pb[1][:, 0:128], bcol[:], ALU.mult)
            P.tt("vector", mt_sb[:], dift[:], dit_[:], ALU.mult, inc="pe1")
            P.nop("vector", inc="ve")
            P.wait("tensor", "ve", P.val("ve"))
            for lvl in range(1, 7):
                bk = lvl % 2
                P.wait("tensor", f"pe{bk}", P.val(f"pe{bk}"))
                P.mm(pb[bk][:, 0:128], ATpow[:, lvl - 1, :],
                     Apow[:, lvl - 1, :], True, True)
                P.mm(pb[bk][:, 128:256], Apow[:, lvl - 1, :],
                     ATpow[:, lvl - 1, :], True, True, inc=f"pf{bk}")
                P.wait("vector", f"pf{bk}", P.val(f"pf{bk}"))
                P.cp("vector", Apow[:, lvl, :], pb[bk][:, 0:128])
                P.cp("vector", ATpow[:, lvl, :], pb[bk][:, 128:256],
                     inc=f"pe{bk}")
                P.nop("vector", inc="ve")
                P.wait("tensor", "ve", P.val("ve"))
            P.cp("vector", s_bf[:, 0, :], s_st[:, hh, 0, :])
            P.cp("vector", s_bf[:, 1, :], s_st[:, hh, 1, :], inc="ve")
            P.wait("tensor", "ve", P.val("ve"))
            P.wait("tensor", "pe2", P.val("pe2"))
            P.mm(pb[2][:, :], k1, s_bf[:, 0, :], True, False)
            P.mm(pb[2][:, :], k2, s_bf[:, 1, :], False, True, inc="pf2")
            P.wait("scalar", "pf2", P.val("pf2"))
            P.act(ks0[:], pb[2][:, :], AF.Copy, inc="pe2")
            P.wait("tensor", "pe7", P.val("pe7"))
            P.tp(pb[7][:, 0:1], lrow[0:1, :], c["ident"][0:1, 0:1], inc="pf7")
            P.wait("scalar", "pf7", P.val("pf7"))
            P.act(nlcol[:], pb[7][:, 0:1], AF.Copy, inc="pe7")
            P.wait("vector", "pe7", P.val("pe7"))
            P.ts("vector", nlcol[:], nlcol[:], -1.0, ALU.mult, inc="ve")
            P.stt("vector", w_sb2[:], ks0[:], nlcol[:], v_tm[:, ci, hh, :],
                  ALU.mult, ALU.add)
            P.cp("vector", u_sb[:], w_sb2[:], inc="ve")
            P.wait("tensor", "ve", P.val("ve"))
            for lvl in range(6, -1, -1):
                bk = 2 + lvl % 2
                lhs = nAT[:] if lvl == 0 else ATpow[:, lvl, :]
                P.wait("tensor", f"pe{bk}", P.val(f"pe{bk}"))
                P.mm(pb[bk][:, :], identb[:], u_sb[:], True, False)
                P.mm(pb[bk][:, :], lhs, u_sb[:], False, True, inc=f"pf{bk}")
                P.wait("vector", f"pf{bk}", P.val(f"pf{bk}"))
                P.cp("vector", u_sb[:], pb[bk][:, :], inc=f"pe{bk}")
                P.nop("vector", inc="ve")
                P.wait("tensor", "ve", P.val("ve"))
            P.wait("vector", "se", P.val("se"))
            P.tt("vector", qlam[:, 0, :], q1, lam_b[:], ALU.mult)
            P.tt("vector", qlam[:, 1, :], q2, lam_b[:], ALU.mult, inc="ve")
            P.wait("tensor", "ve", P.val("ve"))
            P.wait("tensor", "pe4", P.val("pe4"))
            P.mm(pb[4][:, :], qlam[:, 0, :], s_bf[:, 0, :], True, False)
            P.mm(pb[4][:, :], qlam[:, 1, :], s_bf[:, 1, :], False, False)
            P.mm(pb[4][:, :], mt_sb[:], u_sb[:], False, True, inc="pf4")
            oslot = (ci * NH + hh) % 2
            if ci * NH + hh >= 2:
                P.wait("scalar", f"ost{oslot}", 16 * ((ci * NH + hh) // 2))
            P.wait("scalar", "pf4", P.val("pf4"))
            P.act(ost[:, oslot, :], pb[4][:, :], AF.Copy, inc="pe4")
            P.wait("sync", "pe4", P.val("pe4"))
            P.dma("sync", o_o[ci * 128:(ci + 1) * 128,
                             hh * 512:(hh + 1) * 512],
                  ost[:, oslot, :], sem=f"ost{oslot}")
            P.wait("vector", "se", P.val("se"))
            P.tt("vector", fcol[:], fcol[:], bcol[:], ALU.mult, inc="ve")
            P.ts("vector", ktil[:], k_tm[:, ci, hh, :], fcol[:], ALU.mult,
                 inc="ve")
            P.wait("tensor", "pe5", P.val("pe5"))
            P.mm(pb[5][:, 0:1], c["ones_row"][:], lrow[0:1, 127:128], True,
                 True, inc="pf5")
            P.wait("scalar", "pf5", P.val("pf5"))
            P.act(lccol[:], pb[5][:, 0:1], AF.Copy, inc="pe5")
            P.wait_eng("vector", "scalar")
            P.wait("tensor", "ve", P.val("ve"))
            for dkt in range(2):
                bk = 2 + dkt
                P.wait("tensor", f"pe{bk}", P.val(f"pe{bk}"))
                P.mm(pb[bk][:, :], ktil[:, dkt * 128:(dkt + 1) * 128],
                     u_sb[:], True, True, inc=f"pf{bk}")
                P.wait("vector", f"pf{bk}", P.val(f"pf{bk}"))
                P.wait("vector", "se", P.val("se"))
                P.stt("vector", s_st[:, hh, dkt, :], s_st[:, hh, dkt, :],
                      lccol[:], pb[bk][:, :], ALU.mult, ALU.add,
                      inc=f"pe{bk}")
            P.nop("vector", inc="ve")
            P.wait("scalar", "ve", P.val("ve"))
            P.nop("scalar", inc="gr0")
            P.barrier()
    P.wait("sync", "ost0", P.val("ost0"))
    P.wait("sync", "ost1", P.val("ost1"))
    if isinstance(upto, tuple):
        return P
    P.barrier()
    return P



def build_launch5():
    P = Prog()
    nc = P.nc
    x4T = nc.dram_tensor("x4T", [D, TOK], F32, kind="ExternalInput").ap()
    o_i = nc.dram_tensor("o_i", [TOK, 3072], F32, kind="ExternalInput").ap()
    sgT = nc.dram_tensor("sgT", [3072, TOK], F32, kind="ExternalInput").ap()
    nw_i = nc.dram_tensor("nw_i", [1, DV1], F32, kind="ExternalInput").ap()
    WoT = nc.dram_tensor("WoT", [3072, D], F32, kind="ExternalInput").ap()
    Wg1 = nc.dram_tensor("Wg1", [D, NE], F32, kind="ExternalInput").ap()
    W1 = nc.dram_tensor("W1", [NE, D, D], F32, kind="ExternalInput").ap()
    b1 = nc.dram_tensor("b1", [NE, D], F32, kind="ExternalInput").ap()
    outT = nc.dram_tensor("outT", [D, TOK], F32, kind="ExternalOutput").ap()
    x5T = nc.dram_tensor("x5T", [D, TOK], F32).ap()

    xr = nc.alloc_sbuf_tensor("xr", [128, KT, TOK], BF16)
    big = nc.alloc_sbuf_tensor("big", [128, 11264], F32)
    xt = big[:, 0:8192].rearrange("p (s a t) -> p s a t", s=2, a=KT)
    xe = big[:, 0:8192].bitcast(BF16).rearrange("p (a t) -> p a t", a=KT)
    mean_r = big[0:1, 8192:9216]
    msq_r = big[0:1, 9216:10240]
    var_r = big[0:1, 10240:11264]
    rstd_r = big[0:1, 9216:10240]
    mr_r = big[0:1, 8192:9216]
    rows = [mean_r, msq_r, var_r, rstd_r, mr_r]
    l_fm = big[0:NE, 0:1024]
    o_t = big[:, 0:3072]
    sq_t = big[:, 3072:6144]
    sg_t = big[:, 6144:9216].rearrange("p (a t) -> p a t", a=24)
    og = nc.alloc_sbuf_tensor("og", [128, 24, 512], BF16)
    r6 = nc.alloc_sbuf_tensor("r6", [128, 8], F32)
    nw_b = nc.alloc_sbuf_tensor("nw_b", [128, DV1], F32)
    inv512 = nc.alloc_sbuf_tensor("inv512", [128, 1], F32)
    cb = nc.alloc_sbuf_tensor("cb", [128, TOK], F32)
    c_fm = nc.alloc_sbuf_tensor("c_fm", [NE, TOK], F32)
    r_tm = nc.alloc_sbuf_tensor("r_tm", [128, 8, NE], F32)
    s_tm = nc.alloc_sbuf_tensor("s_tm", [128, 8, NE], F32)
    m_tm = nc.alloc_sbuf_tensor("m_tm", [128, 8, NE], F32)
    w_tm = nc.alloc_sbuf_tensor("w_tm", [128, 8, NE], F32)
    r1 = nc.alloc_sbuf_tensor("r1c", [128, 8], F32)
    r2 = nc.alloc_sbuf_tensor("r2c", [128, 8], F32)
    sel8 = nc.alloc_sbuf_tensor("sel8", [NE, NE * 128], F32)
    wg1 = nc.alloc_sbuf_tensor("wg1", [128, KT, NE], BF16)
    b1sb = nc.alloc_sbuf_tensor("b1sb", [NE, D], F32)
    nwrow = nc.alloc_sbuf_tensor("nwrow", [1, DV1], F32)

    c = alloc_common(P)
    pb = c["pb"]
    ws = WStream(P, "wb", cap=16384)
    stq = StoreRot(P, "stg", 4, 512)

    P.mset(sel8[:], 1.0)
    P.mset(inv512[:], 1.0 / DV1, inc="ge")
    P.wait("gpsimd", "ge", P.val("ge"))
    P.asel(sel8[:].rearrange("p (e q) -> p e q", e=NE),
           sel8[:].rearrange("p (e q) -> p e q", e=NE),
           [[-1, NE], [0, 128]], ALU.is_equal, 0.0, 0, 1, inc="ge")

    P.dma("gpsimd", wg1[:], Wg1.rearrange("(a p) m -> p a m", p=128),
          sem="ldc")
    P.dma("sync", b1sb[:], b1[:, :], sem="ld")
    P.dma("sync", nwrow[:], nw_i[:], sem="ld")
    P.dma("gpsimd", x5T[:], x4T[:], sem="xai")
    P.wait("sync", "ld")
    P.wait("gpsimd", "ldc")
    P.wait("gpsimd", "xai", 16)
    P.barrier()
    P.wait("tensor", "pe6", P.val("pe6"))
    P.mm(pb[6][:, :], c["ones_row"][:], nwrow[0:1, :], True, True, inc="pf6")
    P.wait("scalar", "pf6", P.val("pf6"))
    P.act(nw_b[:], pb[6][:, :], AF.Copy, inc="pe6")
    P.barrier()

    for th in range(2):
        for i in range(4):
            ti = th * 4 + i
            tsl = slice(ti * 128, (ti + 1) * 128)
            lsl = slice(i * 128, (i + 1) * 128)
            P.dma("sync", o_t, o_i[tsl, :], sem="lot")
            P.dma("sync", sg_t,
                  sgT.rearrange("(a p) t -> p a t", p=128)[:, :, tsl],
                  sem="lot")
            P.wait("vector", "lot", P.val("lot"))
            P.tt("vector", sq_t, o_t, o_t, ALU.mult, inc="ve")
            P.wait("vector", "ve", P.val("ve"))
            P.red("vector", r6[:, 0:6].unsqueeze(2),
                  sq_t.rearrange("p (h v) -> p h v", h=6), ALU.add, inc="ve")
            P.wait("scalar", "ve", P.val("ve"))
            P.act(r6[:, 0:6], r6[:, 0:6], AF.Ln, scale=inv512[:],
                  bias=c["epsc"][:])
            P.act(r6[:, 0:6], r6[:, 0:6], AF.Exp, scale=c["m05c"][:],
                  inc="se")
            P.wait("vector", "se", P.val("se"))
            for h in range(6):
                P.ts("vector", o_t[:, h * 512:(h + 1) * 512],
                     o_t[:, h * 512:(h + 1) * 512], r6[:, h:h + 1], ALU.mult)
            nb = nw_b[:].unsqueeze(1).broadcast_to([128, 6, DV1])
            P.tt("vector", o_t.rearrange("p (h v) -> p h v", h=6),
                 o_t.rearrange("p (h v) -> p h v", h=6), nb, ALU.mult,
                 inc="ve")
            P.wait("tensor", "ve", P.val("ve"))
            for fc in range(24):
                bk = 6 + fc % 2
                P.wait("tensor", f"pe{bk}", P.val(f"pe{bk}"))
                P.tp(pb[bk][:, 0:128], o_t[:, fc * 128:(fc + 1) * 128],
                     c["ident"][:], inc=f"pf{bk}")
                P.wait("vector", f"pf{bk}", P.val(f"pf{bk}"))
                P.tt("vector", og[:, fc, lsl], pb[bk][:, 0:128],
                     sg_t[:, fc, :], ALU.mult, inc=f"pe{bk}")
            lastbk = 6 + 23 % 2
            P.wait("sync", f"pe{lastbk}", P.val(f"pe{lastbk}"))
        P.barrier()
        specs = [(WoT[:, u * 512:(u + 1) * 512], 24, 512) for u in range(4)]
        units = [ws.load(*specs[0]), ws.load(*specs[1])]

        def rel5(evt, units=units, specs=specs):
            ws.release(evt)
            if len(units) < len(specs):
                units.append(ws.load(*specs[len(units)]))

        for u in range(4):
            unit = units[u]
            ws.consume_wait(unit)
            for sslot in range(4):
                P.wait("gpsimd", f"xa{sslot}", P.val(f"xa{sslot}"))
            for j in range(4):
                bk = j % 4
                P.wait("tensor", f"pe{bk}", P.val(f"pe{bk}"))
                for kt in range(24):
                    P.mm(pb[bk][:, :], ws.tile(unit, kt, j * 128, 128),
                         og[:, kt, :], kt == 0, kt == 23,
                         inc=(f"pf{bk}" if kt == 23 else None))
                slot = stq.uses % stq.n
                stq.pre_evac_wait()
                P.wait("scalar", f"pf{bk}", P.val(f"pf{bk}"))
                P.act(stq.slot(), pb[bk][:, :], AF.Copy, inc=f"pe{bk}")
                r0 = u * 512 + j * 128
                dst = x5T[r0:r0 + 128, th * 512:(th + 1) * 512]
                s = stq.stg[:, slot, :]
                P.wait("gpsimd", f"pe{bk}", P.val(f"pe{bk}"))
                P.dma("gpsimd", dst, s, sem=f"xa{slot}", accum=True)
                stq.slot_evt[slot] = (f"xa{slot}", P.val(f"xa{slot}"))
                stq.uses += 1
            rel5((f"pe{3}", P.val("pe3")))
        for sslot in range(4):
            P.wait("gpsimd", f"xa{sslot}", P.val(f"xa{sslot}"))
        P.barrier()

    P.dma("gpsimd", outT[:], x5T[:], sem="xai")
    P.wait("gpsimd", "xai")
    emit_ln_stream(P, c, x5T, xr, xt, rows, "ln6")
    P.barrier()
    emit_l1_routing(P, c, xr, wg1, l_fm, c_fm, r_tm, s_tm, m_tm, w_tm, r1, r2)
    P.barrier()
    emit_l1_experts(P, c, xr, c_fm, sel8, ws, stq, xe, cb, W1, b1sb, outT)
    for sslot in range(4):
        P.wait("gpsimd", f"xa{sslot}", P.val(f"xa{sslot}"))
    stq.wait_all("sync")
    P.barrier()
    return P



_PROGS = {}


def _prog(name):
    if name not in _PROGS:
        builders = {"L1": build_launch1, "L2": build_launch2,
                    "L3": build_launch3, "L4": build_launch4,
                    "L5": build_launch5}
        _PROGS[name] = builders[name]().finalize()
    return _PROGS[name]


def kernel(**inputs):
    inp = {k: np.ascontiguousarray(np.asarray(v, dtype=np.float32))
           for k, v in inputs.items()}
    cores = list(range(NCORE))

    w1 = _prep_weights_l1(inp)

    nc1 = _prog("L1")
    maps = []
    for cid in cores:
        m = dict(w1)
        m["xT"] = core_token_slice(inp["L0_attn_x"], cid)
        maps.append(m)
    r1 = run_bass_kernel_spmd(nc1, maps, core_ids=cores).results

    x1_f = np.concatenate([r["x1T"] for r in r1], axis=1)
    sv_f = np.concatenate([r["silu_v"] for r in r1], axis=1)
    sxc_f = np.concatenate([r["sxc"] for r in r1], axis=1)
    dB_f = np.concatenate([r["dB_o"] for r in r1], axis=1)
    dlt_f = np.concatenate([r["dlt_o"] for r in r1], axis=1)

    nc2 = _prog("L2")
    maps = []
    for cid in cores:
        rows = slice(cid * 256, (cid + 1) * 256)
        maps.append(dict(
            sxc=np.ascontiguousarray(sxc_f[rows]),
            dB=dB_f, dlt=dlt_f,
            A_log=np.ascontiguousarray(inp["L0_attn_A_log"][rows]),
            C_par=np.ascontiguousarray(inp["L0_attn_C_param"][rows]),
            conv_w=np.ascontiguousarray(inp["L0_attn_conv_w"][rows, 0, :]),
        ))
    r2 = run_bass_kernel_spmd(nc2, maps, core_ids=cores).results
    y_f = np.concatenate([r["y_o"] for r in r2], axis=0)

    nc3 = _prog("L3")
    shared3 = dict(
        gate_wT=w1["gate_wT"], bias8=w1["bias8"], ws1T=w1["ws1T"],
        ws3T=w1["ws3T"], ws2T=w1["ws2T"], we1T=w1["we1T"], we3T=w1["we3T"],
        we2T=w1["we2T"],
        Wg1=np.ascontiguousarray(inp["L1_ffn_Wg"]),
        W1=np.ascontiguousarray(inp["L1_ffn_W"]),
        b1=np.ascontiguousarray(inp["L1_ffn_b"]),
        WqT=np.ascontiguousarray(inp["L1_attn_Wq"].T),
        WkT=np.ascontiguousarray(inp["L1_attn_Wk"].T),
        WvT=np.ascontiguousarray(inp["L1_attn_Wv"].T),
        WgT=np.ascontiguousarray(inp["L1_attn_Wg"].T),
        WbT=np.ascontiguousarray(inp["L1_attn_Wb"].T),
        WaT=np.ascontiguousarray(inp["L1_attn_Wa"].T),
        dtb_i=np.ascontiguousarray(inp["L1_attn_dt_bias"].reshape(6, 1)),
        A6_i=np.ascontiguousarray(inp["L1_attn_A_log"].reshape(6, 1)),
    )
    maps = []
    for cid in cores:
        csl = slice(cid * TOK, (cid + 1) * TOK)
        m = dict(shared3)
        m["x1T"] = np.ascontiguousarray(x1_f[:, csl])
        m["yT"] = np.ascontiguousarray(y_f[:, csl])
        m["svT"] = np.ascontiguousarray(sv_f[:, csl])
        maps.append(m)
    r3 = run_bass_kernel_spmd(nc3, maps, core_ids=cores).results
    x4_f = np.concatenate([r["x4T"] for r in r3], axis=1)
    q_f = np.concatenate([r["q_o"] for r in r3], axis=1)
    k_f = np.concatenate([r["k_o"] for r in r3], axis=1)
    v_f = np.concatenate([r["v_o"] for r in r3], axis=1)
    sg_f = np.concatenate([r["sg_o"] for r in r3], axis=1)
    be_f = np.concatenate([r["beta_o"] for r in r3], axis=1)
    g_f = np.concatenate([r["g_o"] for r in r3], axis=1)

    nc4 = _prog("L4")
    maps = []
    for cid in cores:
        b, ht = cid // 2, cid % 2
        csl = slice(b * S, (b + 1) * S)
        rq = slice(ht * 768, (ht + 1) * 768)
        rv = slice(ht * 1536, (ht + 1) * 1536)
        rh = slice(ht * 3, (ht + 1) * 3)
        maps.append(dict(
            qraw=np.ascontiguousarray(q_f[rq, csl]),
            kraw=np.ascontiguousarray(k_f[rq, csl]),
            vraw=np.ascontiguousarray(v_f[rv, csl]),
            cwq=np.ascontiguousarray(inp["L1_attn_conv_wq"][rq]),
            cwk=np.ascontiguousarray(inp["L1_attn_conv_wk"][rq]),
            cwv=np.ascontiguousarray(inp["L1_attn_conv_wv"][rv]),
            beta_i=np.ascontiguousarray(be_f[rh, csl]),
            g_i=np.ascontiguousarray(g_f[rh, csl]),
        ))
    r4 = run_bass_kernel_spmd(nc4, maps, core_ids=cores).results
    o_f = np.zeros((B * S, 3072), np.float32)
    for cid in cores:
        b, ht = cid // 2, cid % 2
        o_f[b * S:(b + 1) * S, ht * 1536:(ht + 1) * 1536] = r4[cid]["o_o"]

    nc5 = _prog("L5")
    shared5 = dict(
        nw_i=np.ascontiguousarray(inp["L1_attn_norm_w"].reshape(1, DV1)),
        WoT=np.ascontiguousarray(inp["L1_attn_Wo"].T),
        Wg1=shared3["Wg1"], W1=shared3["W1"], b1=shared3["b1"],
    )
    maps = []
    for cid in cores:
        csl = slice(cid * TOK, (cid + 1) * TOK)
        m = dict(shared5)
        m["x4T"] = np.ascontiguousarray(x4_f[:, csl])
        m["o_i"] = np.ascontiguousarray(o_f[cid * TOK:(cid + 1) * TOK, :])
        m["sgT"] = np.ascontiguousarray(sg_f[:, csl])
        maps.append(m)
    r5 = run_bass_kernel_spmd(nc5, maps, core_ids=cores).results
    out_f = np.concatenate([r["outT"] for r in r5], axis=1)
    return np.ascontiguousarray(out_f.T.reshape(B, S, D))
